# version 10 (final)
# speedup vs baseline: 1.0991x; 1.0346x over previous
import numpy as np
from contextlib import ExitStack
import concourse.bass as bass
import concourse.mybir as mybir
from concourse.bass_utils import run_bass_kernel_spmd

F32 = mybir.dt.float32
BF16 = mybir.dt.bfloat16
AF = mybir.ActivationFunctionType
ALU = mybir.AluOpType
AX = mybir.AxisListType.X

T = 4096
D = 1024
NCORE = 8
TC = 512
NTC = T // TC
G = D // 128


def hap(hbm_ap, elem_off, dims):
    return bass.AP(hbm_ap.tensor, hbm_ap.offset + elem_off, dims)


class Sem:
    __slots__ = ('h', 'v', 'base')

    def __init__(self, nc, name):
        self.h = nc.alloc_semaphore(name)
        self.v = 0
        self.base = 0

    def inc(self, instr, n=1):
        instr.then_inc(self.h, n)
        self.v += n
        return self.v


class Prog:
    def __init__(self):
        self.nc = bass.Bass("TRN2", target_bir_lowering=False)
        self.host_inputs = {}
        self._inp_aps = {}
        self.per_core = {}
        self._sem_i = 0
        self._scr_i = 0

    POOL_N = {'c': 28, 'hw': 20, 'sw': 12}

    def sem(self, name='', kind='c'):
        if not hasattr(self, '_sem_pools'):
            self._sem_pools = {'c': [], 'hw': [], 'sw': []}
            self._sem_is = {'c': 0, 'hw': 0, 'sw': 0}
        pool = self._sem_pools[kind]
        i = self._sem_is[kind]
        slot = i % self.POOL_N[kind]
        self._sem_is[kind] = i + 1
        s = Sem.__new__(Sem)
        if slot < len(pool):
            h, prev = pool[slot]
            s.h = h
            s.base = prev.v
            pool[slot][1] = s
        else:
            h = self.nc.alloc_semaphore(f"sp_{kind}{slot}")
            s.h = h
            s.base = 0
            pool.append([h, s])
        s.v = s.base
        return s

    def stage_reset(self):
        pass

    def inp(self, name, arr):
        if name in self.host_inputs:
            return self._inp_aps[name]
        arr = np.ascontiguousarray(np.asarray(arr, np.float32))
        self.host_inputs[name] = arr
        ap = self.nc.dram_tensor(name, list(arr.shape), F32, kind="ExternalInput").ap()
        if not hasattr(self, '_inp_aps'):
            self._inp_aps = {}
        self._inp_aps[name] = ap
        return ap

    def inp_core(self, name, arrs):
        arrs = [np.ascontiguousarray(np.asarray(a, np.float32)) for a in arrs]
        self.per_core[name] = arrs
        return self.nc.dram_tensor(name, list(arrs[0].shape), F32, kind="ExternalInput").ap()

    def out(self, name, shape):
        return self.nc.dram_tensor(name, list(shape), F32, kind="ExternalOutput").ap()

    def scratch(self, shape, dtype=F32, name=None):
        self._scr_i += 1
        return self.nc.dram_tensor(name or f"scr{self._scr_i}", list(shape), dtype,
                                   kind="Internal").ap()

    def in_maps(self, n=None):
        if n is None:
            n = min([len(v) for v in self.per_core.values()], default=NCORE)
        maps = []
        for c in range(n):
            m = dict(self.host_inputs)
            for k, v in self.per_core.items():
                m[k] = v[c]
            maps.append(m)
        return maps


def bias_layout(b):
    b = np.asarray(b, np.float32)
    n = b.shape[0] // 128
    return np.ascontiguousarray(b.reshape(n, 128).T)


CONST_VALS = [1e-5, 0.125, 0.5, -0.5, -0.125]


class Consts:

    def __init__(self, p, ctx):
        nc = p.nc
        e = ctx.enter_context
        ones_row_h = p.inp('c_ones_row', np.ones((1, 128), np.float32))
        ones_col_h = p.inp('c_ones_col', np.ones((128, 1), np.float32))
        eye_h = p.inp('c_eye', np.eye(128, dtype=np.float32))
        cv_h = p.inp('c_cvals', np.array([CONST_VALS], np.float32))
        self.ones_row = e(nc.sbuf_tensor("c_ones_row_s", [1, 128], F32))
        self.ones_row_bf = e(nc.sbuf_tensor("c_ones_row_bf", [1, 128], BF16))
        self.ones_col = e(nc.sbuf_tensor("c_ones_col_s", [128, 1], F32))
        self.ident = e(nc.sbuf_tensor("c_eye_s", [128, 128], BF16))
        self.cvals = e(nc.sbuf_tensor("c_cvals_s", [1, len(CONST_VALS)], F32))
        for i, v in enumerate(CONST_VALS):
            nc.const_aps.register(F32, float(v), self.cvals[0:1, i:i + 1])
        s = p.sem('cld', 'hw')
        with nc.Block(no_gpsimd_drain=True) as blk:
            @blk.sync
            def _(eng):
                s.inc(eng.dma_start(out=self.ones_row[:], in_=ones_row_h), 16)
                s.inc(eng.dma_start(out=self.ones_col[:], in_=ones_col_h), 16)
                s.inc(eng.dma_start(out=self.cvals[:], in_=cv_h), 16)
                eng.wait_ge(s.h, s.v)

            @blk.gpsimd
            def _(eng):
                sg = p.sem('cldg', 'sw')
                sg.inc(eng.dma_start(out=self.ident[:], in_=eye_h), 16)
                sg.inc(eng.dma_start(out=self.ones_row_bf[:], in_=ones_row_h), 16)
                eng.wait_ge(sg.h, sg.v)



def emit_ln(p, ctx, consts, tag, xT, XN, gain_beta=None, src_is_sbuf=False,
            hbm_out=None):
    nc = p.nc
    e = ctx.enter_context
    if src_is_sbuf:
        def xg(g, tc):
            return xT[:, g * T + tc * TC: g * T + (tc + 1) * TC]
    else:
        xb = e(nc.sbuf_tensor(f"{tag}_xb", [128, G * TC], F32))
        def xg(g, tc):
            return xb[:, g * TC:(g + 1) * TC]
    sq = e(nc.sbuf_tensor(f"{tag}_sq", [128, G * TC], F32))
    row = e(nc.sbuf_tensor(f"{tag}_row", [1, 6 * TC], F32))
    ps_s = e(nc.psum_tensor(f"{tag}_pss", [1, TC], F32))
    ps_q = e(nc.psum_tensor(f"{tag}_psq", [1, TC], F32))
    ps_r = e(nc.psum_tensor(f"{tag}_psr", [128, TC], F32))
    ps_m = e(nc.psum_tensor(f"{tag}_psm", [128, TC], F32))
    s_ld = p.sem(tag, 'hw')
    s_v = p.sem(tag)
    s_pe = p.sem(tag)
    s_sc = p.sem(tag)
    if hbm_out is not None:
        ostg = e(nc.sbuf_tensor(f"{tag}_ostg", [128, 2 * G * TC], F32))
        s_so = [p.sem(tag, 'hw'), p.sem(tag, 'hw')]

    with nc.Block(no_gpsimd_drain=True) as blk:
        if not src_is_sbuf:
            @blk.sync
            def _(eng):
                for tc in range(NTC):
                    if tc >= 1:
                        eng.wait_ge(s_v.h, s_v.base + (tc * 4))
                    for g in range(G):
                        i = eng.dma_start(out=xb[:, g * TC:(g + 1) * TC],
                                          in_=xT[g * 128:(g + 1) * 128,
                                                 tc * TC:(tc + 1) * TC])
                        s_ld.inc(i, 16)
                    if hbm_out is not None:
                        eng.wait_ge(s_v.h, s_v.base + (tc * 4 + 4))
                        for g in range(G):
                            i = eng.dma_start(
                                out=hbm_out[g * 128:(g + 1) * 128, tc * TC:(tc + 1) * TC],
                                in_=ostg[:, (tc % 2) * G * TC + g * TC:
                                         (tc % 2) * G * TC + (g + 1) * TC])
                            s_so[tc % 2].inc(i, 16)
                if hbm_out is not None:
                    eng.wait_ge(s_so[0].h, s_so[0].v)
                    eng.wait_ge(s_so[1].h, s_so[1].v)

        @blk.vector
        def _(eng):
            for tc in range(NTC):
                if not src_is_sbuf:
                    eng.wait_ge(s_ld.h, s_ld.base + ((tc + 1)) * G * 16)
                for g in range(G):
                    i = eng.tensor_mul(sq[:, g * TC:(g + 1) * TC], xg(g, tc), xg(g, tc))
                i = eng.drain()
                s_v.inc(i)
                eng.wait_ge(s_pe.h, s_pe.base + (tc * 2 + 1))
                eng.tensor_scalar_mul(row[:, 0:TC], ps_s[:], 1.0 / D)
                eng.tensor_scalar_mul(row[:, TC:2 * TC], ps_q[:], 1.0 / D)
                eng.drain()
                eng.tensor_mul(row[:, 2 * TC:3 * TC], row[:, 0:TC], row[:, 0:TC])
                eng.drain()
                i = eng.tensor_sub(row[:, 2 * TC:3 * TC], row[:, TC:2 * TC],
                                   row[:, 2 * TC:3 * TC])
                i = eng.drain()
                s_v.inc(i)
                eng.wait_ge(s_sc.h, s_sc.base + (tc + 1))
                i = eng.tensor_mul(row[:, 4 * TC:5 * TC], row[:, 0:TC],
                                   row[:, 3 * TC:4 * TC])
                i = eng.drain()
                s_v.inc(i)
                eng.wait_ge(s_pe.h, s_pe.base + (tc * 2 + 2))
                for g in range(G):
                    eng.tensor_mul(sq[:, g * TC:(g + 1) * TC], xg(g, tc), ps_r[:])
                eng.drain()
                if hbm_out is not None and tc >= 2:
                    eng.wait_ge(s_so[tc % 2].h, s_so[tc % 2].base + ((tc // 2)) * G * 16)
                for g in range(G):
                    if hbm_out is not None:
                        dst = ostg[:, (tc % 2) * G * TC + g * TC:
                                   (tc % 2) * G * TC + (g + 1) * TC]
                        eng.tensor_sub(dst, sq[:, g * TC:(g + 1) * TC], ps_m[:])
                        continue
                    dst = XN[:, g * T + tc * TC: g * T + (tc + 1) * TC]
                    if gain_beta is None:
                        eng.tensor_sub(dst, sq[:, g * TC:(g + 1) * TC], ps_m[:])
                    else:
                        eng.tensor_sub(sq[:, g * TC:(g + 1) * TC],
                                       sq[:, g * TC:(g + 1) * TC], ps_m[:])
                if gain_beta is not None:
                    gn, bt = gain_beta
                    eng.drain()
                    for g in range(G):
                        eng.tensor_scalar(XN[:, g * T + tc * TC: g * T + (tc + 1) * TC],
                                          sq[:, g * TC:(g + 1) * TC],
                                          gn[:, g:g + 1], bt[:, g:g + 1],
                                          ALU.mult, ALU.add)
                i = eng.drain()
                s_v.inc(i)

        if hbm_out is not None and src_is_sbuf:
            @blk.sync
            def _(eng):
                for tc in range(NTC):
                    eng.wait_ge(s_v.h, s_v.base + (tc * 4 + 4))
                    for g in range(G):
                        i = eng.dma_start(
                            out=hbm_out[g * 128:(g + 1) * 128, tc * TC:(tc + 1) * TC],
                            in_=ostg[:, (tc % 2) * G * TC + g * TC:
                                     (tc % 2) * G * TC + (g + 1) * TC])
                        s_so[tc % 2].inc(i, 16)
                eng.wait_ge(s_so[0].h, s_so[0].v)
                eng.wait_ge(s_so[1].h, s_so[1].v)

        @blk.tensor
        def _(eng):
            for tc in range(NTC):
                eng.wait_ge(s_v.h, s_v.base + (tc * 4 + 1))
                for g in range(G):
                    eng.matmul(ps_s[:], consts.ones_col[:], xg(g, tc),
                               start=(g == 0), stop=(g == G - 1))
                for g in range(G):
                    i = eng.matmul(ps_q[:], consts.ones_col[:], sq[:, g * TC:(g + 1) * TC],
                                   start=(g == 0), stop=(g == G - 1))
                s_pe.inc(i)
                eng.wait_ge(s_v.h, s_v.base + (tc * 4 + 3))
                eng.matmul(ps_r[:], consts.ones_row[:], row[:, 3 * TC:4 * TC],
                           start=True, stop=True)
                i = eng.matmul(ps_m[:], consts.ones_row[:], row[:, 4 * TC:5 * TC],
                               start=True, stop=True)
                s_pe.inc(i)

        @blk.scalar
        def _(eng):
            for tc in range(NTC):
                eng.wait_ge(s_v.h, s_v.base + (tc * 4 + 2))
                eng.activation(row[:, 5 * TC:6 * TC], row[:, 2 * TC:3 * TC], AF.Ln, bias=1e-5)
                eng.drain()
                i = eng.activation(row[:, 3 * TC:4 * TC], row[:, 5 * TC:6 * TC],
                                   AF.Exp, scale=-0.5)
                eng.drain()
                s_sc.inc(i)



def emit_proj(p, ctx, consts, tag, XN, wT_hbm, Kdim, Mdim, out_cb,
              rhs=None, n_tc=NTC, tcw=TC, packed=False):
    nc = p.nc
    e = ctx.enter_context
    nK = Kdim // 128
    nM = Mdim // 128
    if rhs is None:
        def rhs(k, tc):
            return XN[:, k * T + tc * tcw: k * T + (tc + 1) * tcw]
    wbuf = e(nc.sbuf_tensor(f"{tag}_wsb", [128, 2 * nK * 128], BF16))
    pss = [e(nc.psum_tensor(f"{tag}_ps{i}", [128, tcw], F32)) for i in range(2)]
    s_wl = [p.sem(tag, 'sw'), p.sem(tag, 'sw')]
    s_mm = p.sem(tag)
    s_ev = p.sem(tag)

    with nc.Block(no_gpsimd_drain=True) as blk:
        @blk.gpsimd
        def _(eng):
            for m in range(nM):
                buf = m % 2
                if m >= 2:
                    eng.wait_ge(s_mm.h, s_mm.base + ((m - 1)) * n_tc)
                if packed:
                    i = eng.dma_start(
                        out=wbuf[:, buf * nK * 128:(buf + 1) * nK * 128],
                        in_=hap(wT_hbm, m * 128 * nK * 128,
                                [[nK * 128, 128], [1, nK * 128]]))
                else:
                    i = eng.dma_start(
                        out=wbuf[:, buf * nK * 128:(buf + 1) * nK * 128].rearrange(
                            "p (k j) -> p k j", j=128),
                        in_=hap(wT_hbm, m * 128, [[Mdim, 128], [128 * Mdim, nK], [1, 128]]))
                s_wl[buf].inc(i, 16)

        @blk.tensor
        def _(eng):
            for m in range(nM):
                buf = m % 2
                eng.wait_ge(s_wl[buf].h, s_wl[buf].base + ((m // 2 + 1)) * 16)
                for tc in range(n_tc):
                    j = m * n_tc + tc
                    ps = pss[j % 2]
                    if j >= 2:
                        eng.wait_ge(s_ev.h, s_ev.base + (j - 1))
                    for k in range(nK):
                        i = eng.matmul(ps[:],
                                       wbuf[:, buf * nK * 128 + k * 128:
                                            buf * nK * 128 + (k + 1) * 128],
                                       rhs(k, tc), start=(k == 0), stop=(k == nK - 1))
                    s_mm.inc(i)

        @blk.scalar
        def _(eng):
            for m in range(nM):
                for tc in range(n_tc):
                    j = m * n_tc + tc
                    eng.wait_ge(s_mm.h, s_mm.base + (j + 1))
                    i = out_cb(eng, m, tc, pss[j % 2])
                    s_ev.inc(i)
    return s_ev



def emit_proj_hbm(p, ctx, consts, tag, XN, wT_hbm, Kdim, Mdim, out_hbm_cb,
                  bias_sb=None, act=None, rhs=None, n_tc=NTC, tcw=TC, out_dtype=BF16):
    nc = p.nc
    e = ctx.enter_context
    nK = Kdim // 128
    nM = Mdim // 128
    if rhs is None:
        def rhs(k, tc):
            return XN[:, k * T + tc * tcw: k * T + (tc + 1) * tcw]
    wbuf = e(nc.sbuf_tensor(f"{tag}_wsb", [128, 2 * nK * 128], BF16))
    stg = e(nc.sbuf_tensor(f"{tag}_stg", [128, 2 * tcw], out_dtype))
    pss = [e(nc.psum_tensor(f"{tag}_ps{i}", [128, tcw], F32)) for i in range(2)]
    s_wl = [p.sem(tag, 'sw'), p.sem(tag, 'sw')]
    s_mm = p.sem(tag)
    s_ev = p.sem(tag)
    s_st = [p.sem(tag, 'hw'), p.sem(tag, 'hw')]

    with nc.Block(no_gpsimd_drain=True) as blk:
        @blk.gpsimd
        def _(eng):
            for m in range(nM):
                buf = m % 2
                if m >= 2:
                    eng.wait_ge(s_mm.h, s_mm.base + ((m - 1)) * n_tc)
                i = eng.dma_start(
                    out=wbuf[:, buf * nK * 128:(buf + 1) * nK * 128].rearrange(
                        "p (k j) -> p k j", j=128),
                    in_=hap(wT_hbm, m * 128, [[Mdim, 128], [128 * Mdim, nK], [1, 128]]))
                s_wl[buf].inc(i, 16)

        @blk.tensor
        def _(eng):
            for m in range(nM):
                buf = m % 2
                eng.wait_ge(s_wl[buf].h, s_wl[buf].base + ((m // 2 + 1)) * 16)
                for tc in range(n_tc):
                    j = m * n_tc + tc
                    ps = pss[j % 2]
                    if j >= 2:
                        eng.wait_ge(s_ev.h, s_ev.base + (j - 1))
                    for k in range(nK):
                        i = eng.matmul(ps[:],
                                       wbuf[:, buf * nK * 128 + k * 128:
                                            buf * nK * 128 + (k + 1) * 128],
                                       rhs(k, tc), start=(k == 0), stop=(k == nK - 1))
                    s_mm.inc(i)

        @blk.scalar
        def _(eng):
            for m in range(nM):
                for tc in range(n_tc):
                    j = m * n_tc + tc
                    eng.wait_ge(s_mm.h, s_mm.base + (j + 1))
                    if j >= 2:
                        eng.wait_ge(s_st[j % 2].h, s_st[j % 2].base + ((j // 2)) * 16)
                    kw = {}
                    if bias_sb is not None:
                        kw['bias'] = bias_sb[:, m:m + 1]
                    i = eng.activation(stg[:, (j % 2) * tcw:(j % 2 + 1) * tcw],
                                       pss[j % 2][:], act or AF.Identity, **kw)
                    eng.drain()
                    s_ev.inc(i)

        @blk.sync
        def _(eng):
            for m in range(nM):
                for tc in range(n_tc):
                    j = m * n_tc + tc
                    eng.wait_ge(s_ev.h, s_ev.base + (j + 1))
                    i = eng.dma_start(out=out_hbm_cb(m, tc),
                                      in_=stg[:, (j % 2) * tcw:(j % 2 + 1) * tcw])
                    s_st[j % 2].inc(i, 16)
            eng.wait_ge(s_st[0].h, s_st[0].v)
            eng.wait_ge(s_st[1].h, s_st[1].v)


def emit_proj_residual(p, ctx, consts, tag, wT_hbm, Kdim, rhs, xT_in, xT_out,
                       bias_sb=None, n_tc=NTC, tcw=TC, xT_in2=None, packed=False):
    nc = p.nc
    e = ctx.enter_context
    nK = Kdim // 128
    nM = D // 128
    wbuf = e(nc.sbuf_tensor(f"{tag}_wsb", [128, 2 * nK * 128], BF16))
    stg = e(nc.sbuf_tensor(f"{tag}_stg", [128, 2 * tcw], F32))
    xld = e(nc.sbuf_tensor(f"{tag}_xld", [128, 2 * tcw], F32))
    if xT_in2 is not None:
        xld2 = e(nc.sbuf_tensor(f"{tag}_xld2", [128, 2 * tcw], F32))
    pss = [e(nc.psum_tensor(f"{tag}_ps{i}", [128, tcw], F32)) for i in range(2)]
    s_wl = [p.sem(tag, 'sw'), p.sem(tag, 'sw')]
    s_xl = [p.sem(tag, 'hw'), p.sem(tag, 'hw')]
    s_mm = p.sem(tag)
    s_ev = p.sem(tag)
    s_st = [p.sem(tag, 'hw'), p.sem(tag, 'hw')]

    with nc.Block(no_gpsimd_drain=True) as blk:
        @blk.gpsimd
        def _(eng):
            for m in range(nM):
                buf = m % 2
                if m >= 2:
                    eng.wait_ge(s_mm.h, s_mm.base + ((m - 1)) * n_tc)
                if packed:
                    i = eng.dma_start(
                        out=wbuf[:, buf * nK * 128:(buf + 1) * nK * 128],
                        in_=hap(wT_hbm, m * 128 * nK * 128,
                                [[nK * 128, 128], [1, nK * 128]]))
                else:
                    i = eng.dma_start(
                        out=wbuf[:, buf * nK * 128:(buf + 1) * nK * 128].rearrange(
                            "p (k j) -> p k j", j=128),
                        in_=hap(wT_hbm, m * 128, [[D, 128], [128 * D, nK], [1, 128]]))
                s_wl[buf].inc(i, 16)

        @blk.tensor
        def _(eng):
            for m in range(nM):
                buf = m % 2
                eng.wait_ge(s_wl[buf].h, s_wl[buf].base + ((m // 2 + 1)) * 16)
                for tc in range(n_tc):
                    j = m * n_tc + tc
                    ps = pss[j % 2]
                    if j >= 2:
                        eng.wait_ge(s_ev.h, s_ev.base + (j - 1))
                    for k in range(nK):
                        i = eng.matmul(ps[:],
                                       wbuf[:, buf * nK * 128 + k * 128:
                                            buf * nK * 128 + (k + 1) * 128],
                                       rhs(k, tc), start=(k == 0), stop=(k == nK - 1))
                    s_mm.inc(i)

        @blk.sync
        def _(eng):
            for m in range(nM):
                for tc in range(n_tc):
                    j = m * n_tc + tc
                    if j >= 2:
                        eng.wait_ge(s_ev.h, s_ev.base + (j - 1))
                    i = eng.dma_start(out=xld[:, (j % 2) * tcw:(j % 2 + 1) * tcw],
                                      in_=xT_in[m * 128:(m + 1) * 128, tc * tcw:(tc + 1) * tcw])
                    s_xl[j % 2].inc(i, 16)
                    if xT_in2 is not None:
                        i = eng.dma_start(out=xld2[:, (j % 2) * tcw:(j % 2 + 1) * tcw],
                                          in_=xT_in2[m * 128:(m + 1) * 128,
                                                     tc * tcw:(tc + 1) * tcw])
                        s_xl[j % 2].inc(i, 16)
                    eng.wait_ge(s_ev.h, s_ev.base + (j + 1))
                    i = eng.dma_start(out=xT_out[m * 128:(m + 1) * 128, tc * tcw:(tc + 1) * tcw],
                                      in_=stg[:, (j % 2) * tcw:(j % 2 + 1) * tcw])
                    s_st[j % 2].inc(i, 16)
            eng.wait_ge(s_st[0].h, s_st[0].v)
            eng.wait_ge(s_st[1].h, s_st[1].v)

        @blk.vector
        def _(eng):
            for m in range(nM):
                for tc in range(n_tc):
                    j = m * n_tc + tc
                    eng.wait_ge(s_mm.h, s_mm.base + (j + 1))
                    nxl = 2 if xT_in2 is not None else 1
                    eng.wait_ge(s_xl[j % 2].h, s_xl[j % 2].base + ((j // 2 + 1)) * 16 * nxl)
                    if j >= 2:
                        eng.wait_ge(s_st[j % 2].h, s_st[j % 2].base + ((j // 2)) * 16)
                    ps = pss[j % 2]
                    if bias_sb is not None:
                        i = eng.scalar_tensor_tensor(
                            stg[:, (j % 2) * tcw:(j % 2 + 1) * tcw], ps[:],
                            bias_sb[:, m:m + 1], xld[:, (j % 2) * tcw:(j % 2 + 1) * tcw],
                            ALU.add, ALU.add)
                    else:
                        i = eng.tensor_add(stg[:, (j % 2) * tcw:(j % 2 + 1) * tcw],
                                           ps[:], xld[:, (j % 2) * tcw:(j % 2 + 1) * tcw])
                    if xT_in2 is not None:
                        eng.drain()
                        i = eng.tensor_add(stg[:, (j % 2) * tcw:(j % 2 + 1) * tcw],
                                           stg[:, (j % 2) * tcw:(j % 2 + 1) * tcw],
                                           xld2[:, (j % 2) * tcw:(j % 2 + 1) * tcw])
                    i = eng.drain()
                    s_ev.inc(i)
    return s_st



NEG = -1.0e9


def causal_maskT_phase(phase):
    r = np.arange(128)[:, None]
    f = np.arange(512)[None, :]
    return np.where(f >= phase * 128 + r, 0.0, NEG).astype(np.float32)


def emit_attention(p, ctx, consts, tag, nheads, load_head, YT, masks_sb,
                   sems, scale=0.125):
    nc = p.nc
    s_v, s_pe, s_sc = sems['v'], sems['pe'], sems['sc']
    s_mm, s_sv, s_ev = sems['mm'], sems['sv'], sems['ev']

    for h in range(nheads):
        with ExitStack() as hctx:
            e = hctx.enter_context
            hd_ = load_head(hctx, h)
            q_sb, k_sb, v_sb = hd_['q'], hd_['k'], hd_['v']
            sqq = e(nc.sbuf_tensor(f"{tag}h{h}_sqq", [64, T], F32))
            sqk = e(nc.sbuf_tensor(f"{tag}h{h}_sqk", [64, T], F32))
            mx = e(nc.sbuf_tensor(f"{tag}h{h}_mx", [1, 48], F32))
            negm = e(nc.sbuf_tensor(f"{tag}h{h}_negm", [128, 1], F32))
            vtok = e(nc.sbuf_tensor(f"{tag}h{h}_vt", [128, 32 * 65], BF16))
            pT = e(nc.sbuf_tensor(f"{tag}h{h}_pT", [128, 32 * TC], BF16))
            stmp = e(nc.sbuf_tensor(f"{tag}h{h}_st", [128, 2 * TC], F32))
            rec = e(nc.sbuf_tensor(f"{tag}h{h}_rec", [64, TC], F32))
            yrow = e(nc.sbuf_tensor(f"{tag}h{h}_yr", [1, 2 * TC], F32))
            ps_t = [e(nc.psum_tensor(f"{tag}h{h}_pst{i}", [128, 64], BF16)) for i in range(2)]
            ps_s = [e(nc.psum_tensor(f"{tag}h{h}_pss{i}", [128, TC], F32)) for i in range(2)]
            ps_y = e(nc.psum_tensor(f"{tag}h{h}_psy", [65, TC], F32))
            ps_r = e(nc.psum_tensor(f"{tag}h{h}_psr", [64, TC], F32))
            ps_n = [ps_s[0][0:1, :], ps_s[1][0:1, :]]
            ps_b = ps_s[0][:, 0:1]


            with nc.Block(no_gpsimd_drain=True) as blk:
                @blk.vector
                def _(eng):
                    eng.tensor_mul(sqq[:, 0:T // 2], q_sb[:, 0:T // 2], q_sb[:, 0:T // 2])
                    eng.tensor_mul(sqq[:, T // 2:T], q_sb[:, T // 2:T], q_sb[:, T // 2:T])
                    eng.tensor_mul(sqk[:, 0:T // 2], k_sb[:, 0:T // 2], k_sb[:, 0:T // 2])
                    eng.tensor_mul(sqk[:, T // 2:T], k_sb[:, T // 2:T], k_sb[:, T // 2:T])
                    eng.memset(vtok[:, :].rearrange("p (c w) -> p c w", w=65)[:, :, 64:65], 1.0)

            vB = s_v.v
            peB = s_pe.v
            with nc.Block(no_gpsimd_drain=True) as blk:
                @blk.vector
                def _(eng):
                    for j in range(16):
                        eng.wait_ge(s_pe.h, peB + j + 1)
                        i = eng.reduce_max(mx[:, j:j + 1], ps_n[j % 2], axis=AX)
                        i = eng.drain()
                        s_v.inc(i)
                    eng.reduce_max(mx[:, 32:33], mx[:, 0:8], axis=AX)
                    eng.drain()
                    eng.reduce_max(mx[:, 33:34], mx[:, 8:16], axis=AX)
                    eng.drain()

                @blk.tensor
                def _(eng):
                    for j in range(16):
                        sq = sqq if j < 8 else sqk
                        c = j % 8
                        if j >= 2:
                            eng.wait_ge(s_v.h, vB + j - 1)
                        i = eng.matmul(ps_n[j % 2][:], consts.ones_col[0:64, :],
                                       sq[:, c * TC:(c + 1) * TC], start=True, stop=True)
                        s_pe.inc(i)

            scC = s_sc.v
            with nc.Block(no_gpsimd_drain=True) as blk:
                @blk.scalar
                def _(eng):
                    eng.activation(mx[:, 34:35], mx[:, 32:33], AF.Ln)
                    eng.drain()
                    eng.activation(mx[:, 35:36], mx[:, 33:34], AF.Ln)
                    eng.drain()
                    eng.activation(mx[:, 36:37], mx[:, 34:35], AF.Exp, scale=0.5)
                    eng.drain()
                    i = eng.activation(mx[:, 37:38], mx[:, 35:36], AF.Exp, scale=0.5)
                    eng.drain()
                    s_sc.inc(i)

                @blk.vector
                def _(eng):
                    eng.wait_ge(s_sc.h, scC + 1)
                    eng.tensor_mul(mx[:, 38:39], mx[:, 36:37], mx[:, 37:38])
                    eng.drain()
                    eng.tensor_scalar_mul(mx[:, 39:40], mx[:, 38:39], -scale)

            vD = s_v.v
            peD = s_pe.v
            with nc.Block(no_gpsimd_drain=True) as blk:
                @blk.tensor
                def _(eng):
                    i = eng.matmul(ps_b, consts.ones_row[:], mx[:, 39:40],
                                   start=True, stop=True)
                    s_pe.inc(i)
                    for c in range(32):
                        if c >= 2:
                            eng.wait_ge(s_v.h, vD + c - 1)
                        i = eng.transpose(ps_t[c % 2][:], v_sb[:, c * 128:(c + 1) * 128],
                                          consts.ident[0:64, 0:64])
                        s_pe.inc(i)

                @blk.vector
                def _(eng):
                    eng.wait_ge(s_pe.h, peD + 1)
                    eng.tensor_copy(negm[:], ps_b)
                    for c in range(32):
                        eng.wait_ge(s_pe.h, peD + c + 2)
                        i = eng.tensor_copy(
                            vtok[:, c * 65:c * 65 + 64], ps_t[c % 2][:])
                        i = eng.drain()
                        s_v.inc(i)

            mmE = s_mm.v
            evE = s_ev.v
            svE = s_sv.v
            vE = s_v.v
            peE = s_pe.v
            scE = s_sc.v
            chunk_of = []
            for qc in range(NTC):
                for kc in range(4 * qc + 4):
                    chunk_of.append((qc, kc, kc >= 4 * qc))
            NCH = len(chunk_of)

            with nc.Block(no_gpsimd_drain=True) as blk:
                @blk.tensor
                def _(eng):
                    jj = 0
                    for qc in range(NTC):
                        nk = 4 * qc + 4
                        for kc in range(nk):
                            if jj >= 2:
                                eng.wait_ge(s_ev.h, evE + jj - 1)
                            i = eng.matmul(ps_s[jj % 2][:],
                                           k_sb[:, kc * 128:(kc + 1) * 128],
                                           q_sb[:, qc * TC:(qc + 1) * TC],
                                           start=True, stop=True)
                            s_mm.inc(i)
                            jj += 1
                        if qc >= 1:
                            eng.wait_ge(s_v.h, vE + qc * 6)
                        for kc in range(nk):
                            eng.wait_ge(s_ev.h, evE + jj - nk + kc + 1)
                            i = eng.matmul(ps_y[:], vtok[:, kc * 65:(kc + 1) * 65],
                                           pT[:, kc * TC:(kc + 1) * TC],
                                           start=(kc == 0), stop=(kc == nk - 1))
                        s_sv.inc(i)
                        eng.wait_ge(s_v.h, vE + qc * 6 + 5)
                        i = eng.matmul(ps_r[:], consts.ones_row[:, 0:64],
                                       yrow[:, TC:TC + TC], start=True, stop=True)
                        s_pe.inc(i)

                @blk.vector
                def _(eng):
                    jj = 0
                    for qc in range(NTC):
                        nk = 4 * qc + 4
                        for kc in range(nk):
                            diag = kc >= 4 * qc
                            if diag:
                                eng.wait_ge(s_mm.h, mmE + jj + 1)
                                if jj >= 2:
                                    eng.wait_ge(s_ev.h, evE + jj - 1)
                                ph = kc - 4 * qc
                                i = eng.scalar_tensor_tensor(
                                    stmp[:, (jj % 2) * TC:(jj % 2 + 1) * TC],
                                    ps_s[jj % 2][:], 1.0,
                                    masks_sb[:, ph * TC:(ph + 1) * TC],
                                    ALU.mult, ALU.add)
                                i = eng.drain()
                                s_v.inc(i)
                            jj += 1
                        eng.wait_ge(s_sv.h, svE + qc + 1)
                        eng.tensor_scalar_add(yrow[:, 0:TC], ps_y[64:65, :], 1e-30)
                        eng.drain()
                        i = eng.reciprocal(yrow[:, TC:2 * TC], yrow[:, 0:TC])
                        i = eng.drain()
                        s_v.inc(i)
                        eng.wait_ge(s_sc.h, scE + qc + 1)
                        r0 = (h % 2) * 64
                        gt = h // 2
                        i = eng.tensor_mul(
                            YT[r0:r0 + 64, gt * T + qc * TC: gt * T + (qc + 1) * TC],
                            ps_y[0:64, :], rec[:, :])
                        i = eng.drain()
                        s_v.inc(i)

                @blk.scalar
                def _(eng):
                    jj = 0
                    for qc in range(NTC):
                        nk = 4 * qc + 4
                        for kc in range(nk):
                            diag = kc >= 4 * qc
                            if qc >= 1 and kc == 0:
                                eng.wait_ge(s_sv.h, svE + qc)
                            if diag:
                                eng.wait_ge(s_v.h, vE + qc * 6 + (kc - 4 * qc) + 1)
                                src = stmp[:, (jj % 2) * TC:(jj % 2 + 1) * TC]
                            else:
                                eng.wait_ge(s_mm.h, mmE + jj + 1)
                                src = ps_s[jj % 2][:]
                            i = eng.activation(pT[:, kc * TC:(kc + 1) * TC], src,
                                               AF.Exp, bias=negm[:], scale=scale)
                            eng.drain()
                            s_ev.inc(i)
                            jj += 1
                        eng.wait_ge(s_pe.h, peE + qc + 1)
                        i = eng.activation(rec[:, :], ps_r[:], AF.Copy)
                        eng.drain()
                        s_sc.inc(i)


def load_small_sbuf(p, ctx, tag, items):
    nc = p.nc
    e = ctx.enter_context
    outs = []
    for idx, (src, shape, dt) in enumerate(items):
        outs.append(e(nc.sbuf_tensor(f"{tag}_c{idx}", list(shape), dt)))
    s = p.sem(tag, 'hw')
    sg = p.sem(tag, 'sw')
    with nc.Block(no_gpsimd_drain=True) as blk:
        @blk.sync
        def _(eng):
            n = 0
            for (src, shape, dt), dst in zip(items, outs):
                if dt == F32:
                    s.inc(eng.dma_start(out=dst[:], in_=src), 16)
                    n += 1
            if n:
                eng.wait_ge(s.h, s.v)

        @blk.gpsimd
        def _(eng):
            n = 0
            for (src, shape, dt), dst in zip(items, outs):
                if dt != F32:
                    sg.inc(eng.dma_start(out=dst[:], in_=src), 16)
                    n += 1
            if n:
                eng.wait_ge(sg.h, sg.v)
    return outs


def rope_tables():
    half = 32
    freqs = (1.0 / 10000.0 ** (np.arange(half, dtype=np.float64) / half))
    ang = np.arange(T, dtype=np.float64)[:, None] * freqs[None, :]
    sin = np.sin(ang).astype(np.float32).T
    cos = np.cos(ang).astype(np.float32).T
    CC = np.concatenate([cos, cos, cos, cos], 0)
    SS = np.concatenate([-sin, sin, -sin, sin], 0)
    perm = np.zeros((128, 128), np.float32)
    for r in range(128):
        k = r + 32 if (r % 64) < 32 else r - 32
        perm[k, r] = 1.0
    return CC, SS, perm


def emit_rope(p, ctx, consts, tag, qkvT, CC_sb, SS_sb, perm_sb):
    nc = p.nc
    e = ctx.enter_context
    tile = e(nc.sbuf_tensor(f"{tag}_t", [128, 2 * T], BF16))
    rot = e(nc.sbuf_tensor(f"{tag}_r", [128, 2 * TC], BF16))
    t1 = e(nc.sbuf_tensor(f"{tag}_1", [128, TC], BF16))
    t2 = e(nc.sbuf_tensor(f"{tag}_2", [128, TC], BF16))
    pss = [e(nc.psum_tensor(f"{tag}_ps{i}", [128, TC], F32)) for i in range(2)]
    s_ld = p.sem(tag, 'hw')
    s_pe = p.sem(tag)
    s_v = p.sem(tag)
    s_st = [p.sem(tag, 'hw'), p.sem(tag, 'hw')]
    nMT = 16

    with nc.Block(no_gpsimd_drain=True) as blk:
        @blk.sync
        def _(eng):
            for m in range(nMT):
                if m >= 2:
                    eng.wait_ge(s_v.h, s_v.base + ((m - 1)) * NTC)
                i = eng.dma_start(out=tile[:, (m % 2) * T:(m % 2 + 1) * T],
                                  in_=qkvT[m * 128:(m + 1) * 128, :])
                s_ld.inc(i, 16)
                for tc in range(NTC):
                    j = m * NTC + tc
                    eng.wait_ge(s_v.h, s_v.base + (j + 1))
                    i = eng.dma_start(out=qkvT[m * 128:(m + 1) * 128, tc * TC:(tc + 1) * TC],
                                      in_=rot[:, (j % 2) * TC:(j % 2 + 1) * TC])
                    s_st[j % 2].inc(i, 16)
            eng.wait_ge(s_st[0].h, s_st[0].v)
            eng.wait_ge(s_st[1].h, s_st[1].v)

        @blk.tensor
        def _(eng):
            for m in range(nMT):
                eng.wait_ge(s_ld.h, s_ld.base + ((m + 1)) * 16)
                for tc in range(NTC):
                    j = m * NTC + tc
                    if j >= 2:
                        eng.wait_ge(s_v.h, s_v.base + (j - 1))
                    i = eng.matmul(pss[j % 2][:], perm_sb[:],
                                   tile[:, (m % 2) * T + tc * TC:(m % 2) * T + (tc + 1) * TC],
                                   start=True, stop=True)
                    s_pe.inc(i)

        @blk.vector
        def _(eng):
            for m in range(nMT):
                for tc in range(NTC):
                    j = m * NTC + tc
                    eng.wait_ge(s_pe.h, s_pe.base + (j + 1))
                    if j >= 2:
                        eng.wait_ge(s_st[j % 2].h, s_st[j % 2].base + ((j // 2)) * 16)
                    eng.tensor_mul(t1[:],
                                   tile[:, (m % 2) * T + tc * TC:(m % 2) * T + (tc + 1) * TC],
                                   CC_sb[:, tc * TC:(tc + 1) * TC])
                    eng.tensor_mul(t2[:], pss[j % 2][:], SS_sb[:, tc * TC:(tc + 1) * TC])
                    eng.drain()
                    i = eng.tensor_add(rot[:, (j % 2) * TC:(j % 2 + 1) * TC], t1[:], t2[:])
                    i = eng.drain()
                    s_v.inc(i)


def emit_l1a(p, ctx, consts, tag, xT_in, xT_out, W):
    nc = p.nc
    qkv_wT_h = p.inp(f'{tag}_qkv_wT', np.ascontiguousarray(W['qkv_w'].T))
    qkv_b_h = p.inp(f'{tag}_qkv_b', bias_layout(W['qkv_b']))
    proj_wT_h = p.inp(f'{tag}_proj_wT', np.ascontiguousarray(W['proj_w'].T))
    proj_b_h = p.inp(f'{tag}_proj_b', bias_layout(W['proj_b']))
    CC, SS, perm = rope_tables()
    CC_h = p.inp('c_rope_cc', CC)
    SS_h = p.inp('c_rope_ss', SS)
    perm_h = p.inp('c_rope_perm', perm)
    masks_h = p.inp('c_maskT', np.concatenate([causal_maskT_phase(i) for i in range(4)], 1))

    qkvT = p.scratch((3072, T), BF16)

    with ExitStack() as c1:
        XN = c1.enter_context(nc.sbuf_tensor(f"{tag}_xn", [128, G * T], BF16))
        emit_ln(p, c1, consts, f"{tag}_ln", xT_in, XN)
        (qkv_b_sb,) = load_small_sbuf(p, c1, f"{tag}_b1", [(qkv_b_h, (128, 24), F32)])
        emit_proj_hbm(p, c1, consts, f"{tag}_qkv", XN, qkv_wT_h, 1024, 3072,
                      lambda m, tc: qkvT[m * 128:(m + 1) * 128, tc * TC:(tc + 1) * TC],
                      bias_sb=qkv_b_sb)

    with ExitStack() as c2:
        (CC_sb, SS_sb, perm_sb) = load_small_sbuf(
            p, c2, f"{tag}_rc", [(CC_h, (128, T), BF16), (SS_h, (128, T), BF16),
                                 (perm_h, (128, 128), BF16)])
        emit_rope(p, c2, consts, f"{tag}_rope", qkvT, CC_sb, SS_sb, perm_sb)

    with ExitStack() as c3:
        YT = c3.enter_context(nc.sbuf_tensor(f"{tag}_yt", [128, G * T], BF16))
        (masks_sb,) = load_small_sbuf(p, c3, f"{tag}_mk", [(masks_h, (128, 4 * TC), F32)])
        s_hl = p.sem(tag, 'hw')

        def load_head(hctx, h):
            e = hctx.enter_context
            q_sb = e(nc.sbuf_tensor(f"{tag}h{h}_q", [64, T], BF16))
            k_sb = e(nc.sbuf_tensor(f"{tag}h{h}_k", [64, T], BF16))
            v_sb = e(nc.sbuf_tensor(f"{tag}h{h}_v", [64, T], BF16))
            with nc.Block(no_gpsimd_drain=True) as blk:
                @blk.sync
                def _(eng):
                    s_hl.inc(eng.dma_start(out=q_sb[:], in_=qkvT[h * 64:(h + 1) * 64, :]), 16)
                    s_hl.inc(eng.dma_start(out=k_sb[:], in_=qkvT[1024 + h * 64:1024 + (h + 1) * 64, :]), 16)
                    s_hl.inc(eng.dma_start(out=v_sb[:], in_=qkvT[2048 + h * 64:2048 + (h + 1) * 64, :]), 16)
                    eng.wait_ge(s_hl.h, s_hl.v)
            return {'q': q_sb, 'k': k_sb, 'v': v_sb}

        sems = {k: p.sem('at' + k) for k in ('v', 'pe', 'sc', 'mm', 'sv', 'ev')}
        emit_attention(p, c3, consts, f"{tag}_at", 16, load_head, YT, masks_sb, sems)

        (proj_b_sb,) = load_small_sbuf(p, c3, f"{tag}_b2", [(proj_b_h, (128, 8), F32)])
        emit_proj_residual(p, c3, consts, f"{tag}_pr", proj_wT_h, 1024,
                           lambda k, tc: YT[:, k * T + tc * TC: k * T + (tc + 1) * TC],
                           xT_in, xT_out, bias_sb=proj_b_sb)


def emit_l0a(p, ctx, consts, tag, xT_in, xT_out, W):
    nc = p.nc
    qkv_w_h = p.inp(f'{tag}_qkv_w', W['Wqkv'])
    qkv_b_h = p.inp(f'{tag}_qkv_b', bias_layout(W['bqkv']))
    proj_wT_h = p.inp(f'{tag}_proj_w', W['Wproj'])
    proj_b_h = p.inp(f'{tag}_proj_b', bias_layout(W['bproj']))
    masks_h = p.inp('c_maskT', np.concatenate([causal_maskT_phase(i) for i in range(4)], 1))

    xpT = p.scratch((3072, T), BF16)

    with ExitStack() as c1:
        XN = c1.enter_context(nc.sbuf_tensor(f"{tag}_xn", [128, G * T], BF16))
        emit_ln(p, c1, consts, f"{tag}_ln", xT_in, XN)
        (qkv_b_sb,) = load_small_sbuf(p, c1, f"{tag}_b1", [(qkv_b_h, (128, 24), F32)])
        emit_proj_hbm(p, c1, consts, f"{tag}_qkv", XN, qkv_w_h, 1024, 3072,
                      lambda m, tc: xpT[m * 128:(m + 1) * 128, tc * TC:(tc + 1) * TC],
                      bias_sb=qkv_b_sb)

    with ExitStack() as c3:
        YT = c3.enter_context(nc.sbuf_tensor(f"{tag}_yt", [128, G * T], BF16))
        (masks_sb,) = load_small_sbuf(p, c3, f"{tag}_mk", [(masks_h, (128, 4 * TC), F32)])
        s_hl = p.sem(tag, 'hw')

        def load_head(hctx, h):
            e = hctx.enter_context
            perm = e(nc.sbuf_tensor(f"{tag}h{h}_pm", [64, 3 * T], BF16))
            q_sb = e(nc.sbuf_tensor(f"{tag}h{h}_q", [64, T], BF16))
            k_sb = e(nc.sbuf_tensor(f"{tag}h{h}_k", [64, T], BF16))
            v_sb = e(nc.sbuf_tensor(f"{tag}h{h}_v", [64, T], BF16))
            with nc.Block(no_gpsimd_drain=True) as blk:
                @blk.sync
                def _(eng):
                    for sec in range(3):
                        src = hap(xpT, sec * 1024 * T + h * 256,
                                  [[T, 64], [64 * T, 16], [1, 256]])
                        i = eng.dma_start(
                            out=perm[:, sec * T:(sec + 1) * T].rearrange(
                                "p (a b) -> p a b", a=16), in_=src)
                        s_hl.inc(i, 16)
                    eng.wait_ge(s_hl.h, s_hl.v)

                @blk.vector
                def _(eng):
                    eng.wait_ge(s_hl.h, s_hl.v)
                    for sec, dst in ((0, q_sb), (1, k_sb), (2, v_sb)):
                        eng.tensor_copy(
                            dst[:, :].rearrange("p (a b) -> p b a", b=16),
                            perm[:, sec * T:(sec + 1) * T].rearrange(
                                "p (a b) -> p a b", a=16))
            return {'q': q_sb, 'k': k_sb, 'v': v_sb}

        sems = {k: p.sem('a0' + k) for k in ('v', 'pe', 'sc', 'mm', 'sv', 'ev')}
        emit_attention(p, c3, consts, f"{tag}_at", 16, load_head, YT, masks_sb, sems)

        (proj_b_sb,) = load_small_sbuf(p, c3, f"{tag}_b2", [(proj_b_h, (128, 8), F32)])
        emit_proj_residual(p, c3, consts, f"{tag}_pr", proj_wT_h, 1024,
                           lambda k, tc: YT[:, k * T + tc * TC: k * T + (tc + 1) * TC],
                           xT_in, xT_out, bias_sb=proj_b_sb)


def emit_l1b(p, ctx, consts, tag, xT_in, xT_out, W):
    nc = p.nc
    bwT = np.ascontiguousarray(W['basis_w'].T)
    basis_wT_h = p.inp(f'{tag}_basis_wT', np.ascontiguousarray(
        bwT.reshape(8, 128, 64).transpose(1, 0, 2).reshape(128, 8 * 64)))
    core_h = p.inp(f'{tag}_core', W['core'])
    hr_h = p.inp(f'{tag}_hr', np.ascontiguousarray(
        np.transpose(W['head_residual'], (1, 0, 2)).reshape(64, 16 * 64)))
    v_wT_h = p.inp(f'{tag}_v_wT', np.ascontiguousarray(W['v_w'].T))
    v_b_h = p.inp(f'{tag}_v_b', bias_layout(W['v_b']))
    o_wT_h = p.inp(f'{tag}_o_wT', np.ascontiguousarray(W['o_w'].T))
    o_b_h = p.inp(f'{tag}_o_b', bias_layout(W['o_b']))
    masks_h = p.inp('c_maskT', np.concatenate([causal_maskT_phase(i) for i in range(4)], 1))

    vT = p.scratch((1024, T), BF16)
    latT = ctx.enter_context(nc.sbuf_tensor(f"{tag}_latT", [64, T], BF16))
    hm_sb = ctx.enter_context(nc.sbuf_tensor(f"{tag}_hm", [64, 16 * 64], BF16))

    with ExitStack() as c1:
        XN = c1.enter_context(nc.sbuf_tensor(f"{tag}_xn", [128, G * T], BF16))
        emit_ln(p, c1, consts, f"{tag}_ln", xT_in, XN)
        (v_b_sb,) = load_small_sbuf(p, c1, f"{tag}_b1", [(v_b_h, (128, 8), F32)])
        emit_proj_hbm(p, c1, consts, f"{tag}_v", XN, v_wT_h, 1024, 1024,
                      lambda m, tc: vT[m * 128:(m + 1) * 128, tc * TC:(tc + 1) * TC],
                      bias_sb=v_b_sb)
        (core_sb, hr_sb, bw_sb) = load_small_sbuf(
            p, c1, f"{tag}_cs", [(core_h, (64, 64), F32), (hr_h, (64, 16 * 64), F32),
                                 (basis_wT_h, (128, 8 * 64), BF16)])
        ps_l = c1.enter_context(nc.psum_tensor(f"{tag}_psl", [64, TC], F32))
        ps_c = c1.enter_context(nc.psum_tensor(f"{tag}_psc", [64, 64], BF16))
        s_mm = p.sem(tag)
        s_ev = p.sem(tag)
        with nc.Block(no_gpsimd_drain=True) as blk:
            @blk.tensor
            def _(eng):
                for tc in range(NTC):
                    if tc >= 1:
                        eng.wait_ge(s_ev.h, s_ev.base + (tc))
                    for k in range(G):
                        i = eng.matmul(ps_l[:], bw_sb[:, k * 64:(k + 1) * 64],
                                       XN[:, k * T + tc * TC: k * T + (tc + 1) * TC],
                                       start=(k == 0), stop=(k == G - 1))
                    s_mm.inc(i)
            @blk.scalar
            def _(eng):
                for tc in range(NTC):
                    eng.wait_ge(s_mm.h, s_mm.base + (tc + 1))
                    i = eng.activation(latT[:, tc * TC:(tc + 1) * TC], ps_l[:], AF.Copy)
                    eng.drain()
                    s_ev.inc(i)

        csc = c1.enter_context(nc.sbuf_tensor(f"{tag}_csc", [64, 3 * 64], F32))
        core_bf = c1.enter_context(nc.sbuf_tensor(f"{tag}_cbf", [64, 64], BF16))
        s_v2 = p.sem(tag)
        s_pe2 = p.sem(tag)
        with nc.Block(no_gpsimd_drain=True) as blk:
            @blk.vector
            def _(eng):
                i = eng.tensor_copy(core_bf[:], core_sb[:])
                i = eng.drain()
                s_v2.inc(i)
                eng.tensor_add(csc[:, 64:128], hr_sb[:, 0:64], hr_sb[:, 64:128])
                for hh in range(2, 16):
                    eng.drain()
                    eng.tensor_add(csc[:, 64:128], csc[:, 64:128],
                                   hr_sb[:, hh * 64:(hh + 1) * 64])
                eng.drain()
                eng.tensor_scalar_mul(csc[:, 64:128], csc[:, 64:128], 1.0 / 16.0)
                eng.drain()
                eng.wait_ge(s_pe2.h, s_pe2.v + 1)
                eng.tensor_add(csc[:, 0:64], core_sb[:], ps_c[:])
                eng.drain()
                eng.tensor_scalar_mul(csc[:, 0:64], csc[:, 0:64], 0.5 / 16.0)
                eng.drain()
                eng.tensor_sub(csc[:, 128:192], csc[:, 0:64], csc[:, 64:128])
                eng.drain()
                for hh in range(16):
                    eng.tensor_add(hm_sb[:, hh * 64:(hh + 1) * 64],
                                   hr_sb[:, hh * 64:(hh + 1) * 64], csc[:, 128:192])

            @blk.tensor
            def _(eng):
                eng.wait_ge(s_v2.h, s_v2.v)
                i = eng.transpose(ps_c[:], core_bf[:], consts.ident[0:64, 0:64])
                s_pe2.inc(i)

    with ExitStack() as c3:
        YT = c3.enter_context(nc.sbuf_tensor(f"{tag}_yt", [128, G * T], BF16))
        (masks_sb,) = load_small_sbuf(p, c3, f"{tag}_mk", [(masks_h, (128, 4 * TC), F32)])
        s_hl = p.sem(tag)

        s_m = p.sem(tag)
        s_e = p.sem(tag)

        def load_head(hctx, h):
            e = hctx.enter_context
            q_sb = e(nc.sbuf_tensor(f"{tag}h{h}_q", [64, T], BF16))
            v_sb = e(nc.sbuf_tensor(f"{tag}h{h}_v", [64, T], BF16))
            ps_q = e(nc.psum_tensor(f"{tag}h{h}_psq2", [64, TC], F32))
            mb = s_m.v
            eb = s_e.v
            with nc.Block(no_gpsimd_drain=True) as blk:
                @blk.sync
                def _(eng):
                    s_hl.inc(eng.dma_start(out=v_sb[:], in_=vT[h * 64:(h + 1) * 64, :]), 16)
                    eng.wait_ge(s_hl.h, s_hl.v)

                @blk.tensor
                def _(eng):
                    for tc in range(NTC):
                        if tc >= 1:
                            eng.wait_ge(s_e.h, eb + tc)
                        i = eng.matmul(ps_q[:], hm_sb[:, h * 64:(h + 1) * 64],
                                       latT[:, tc * TC:(tc + 1) * TC],
                                       start=True, stop=True)
                        s_m.inc(i)

                @blk.scalar
                def _(eng):
                    for tc in range(NTC):
                        eng.wait_ge(s_m.h, mb + tc + 1)
                        i = eng.activation(q_sb[:, tc * TC:(tc + 1) * TC], ps_q[:], AF.Copy)
                        eng.drain()
                        s_e.inc(i)
            return {'q': q_sb, 'k': latT, 'v': v_sb}

        sems = {k: p.sem('ab' + k) for k in ('v', 'pe', 'sc', 'mm', 'sv', 'ev')}
        emit_attention(p, c3, consts, f"{tag}_at", 16, load_head, YT, masks_sb, sems)

        (o_b_sb,) = load_small_sbuf(p, c3, f"{tag}_b2", [(o_b_h, (128, 8), F32)])
        emit_proj_residual(p, c3, consts, f"{tag}_pr", o_wT_h, 1024,
                           lambda k, tc: YT[:, k * T + tc * TC: k * T + (tc + 1) * TC],
                           xT_in, xT_out, bias_sb=o_b_sb)


MOE1_TRUNC = None


def _dummy_copy(p, ctx, consts, tag, xT_in, xT_out):
    nc = p.nc
    with ExitStack() as c:
        t = c.enter_context(nc.sbuf_tensor(f"{tag}_t", [128, T], F32))
        s = p.sem(tag, 'hw')
        with nc.Block(no_gpsimd_drain=True) as blk:
            @blk.sync
            def _(eng):
                for g in range(G):
                    i = eng.dma_start(out=t[:], in_=xT_in[g * 128:(g + 1) * 128, :])
                    s.inc(i, 16)
                    eng.wait_ge(s.h, s.v)
                    i = eng.dma_start(out=xT_out[g * 128:(g + 1) * 128, :], in_=t[:])
                    s.inc(i, 16)
                    eng.wait_ge(s.h, s.v)


def emit_moe1(p, ctx, consts, tag, xT_in, xT_out, W, wname, dbg=None):
    nc = p.nc
    E, H = 8, 4096
    gate_wT_h = p.inp(f'{wname}_gwT', np.ascontiguousarray(
        W['gate_w'].T.reshape(8, 128, 8).transpose(1, 0, 2).reshape(128, 64)))
    gate_b_h = p.inp(f'{wname}_gb', W['gate_b'].reshape(1, 8))
    w1T_h = p.inp(f'{wname}_w1R', np.ascontiguousarray(
        W['w1'].reshape(8, 32, 128, 8, 128).transpose(0, 1, 4, 3, 2)))
    b1_h = p.inp(f'{wname}_b1', np.concatenate(
        [bias_layout(W['b1'][i]) for i in range(E)], 1))
    w2T_h = p.inp(f'{wname}_w2R', np.ascontiguousarray(
        W['w2'].reshape(8, 8, 128, 32, 128).transpose(0, 1, 4, 3, 2)))
    b2_h = p.inp(f'{wname}_b2', W['b2'])
    iota8_h = p.inp('c_iota8', np.broadcast_to(np.arange(8, np.float32)
                    if False else np.arange(8, dtype=np.float32), (128, 8)).copy())

    e = ctx.enter_context
    XN = e(nc.sbuf_tensor(f"{tag}_xn", [128, G * T], BF16))
    with ExitStack() as cL:
        emit_ln(p, cL, consts, f"{tag}_ln", xT_in, XN)
    (gwT_sb, gb_sb, io8_sb, b1_sb, b2_sb) = load_small_sbuf(
        p, ctx, f"{tag}_sc", [(gate_wT_h, (128, G * 8), BF16), (gate_b_h, (1, 8), BF16),
                              (iota8_h, (128, 8), F32), (b1_h, (128, E * 32), F32),
                              (b2_h, (8, 1024), BF16)])
    route = e(nc.sbuf_tensor(f"{tag}_rt", [9, T], BF16))
    sel9_h = p.inp('c_sel9', np.eye(9, dtype=np.float32).repeat(128, 1).reshape(9, 9 * 128)
                   if False else np.kron(np.eye(9, dtype=np.float32), np.ones((1, 128), np.float32)))
    (sel9_sb,) = load_small_sbuf(p, ctx, f"{tag}_s9", [(sel9_h, (9, 9 * 128), BF16)])

    with ExitStack() as cR:
        er = cR.enter_context
        sc = er(nc.sbuf_tensor(f"{tag}_scb", [128, 2 * 16], F32))
        e8 = er(nc.sbuf_tensor(f"{tag}_e8", [128, 16], F32))
        m8 = er(nc.sbuf_tensor(f"{tag}_m8", [128, 16], F32))
        rts = er(nc.sbuf_tensor(f"{tag}_rts", [128, 2 * 16], BF16))
        ps_sc = [er(nc.psum_tensor(f"{tag}_pssc{i}", [128, 8], F32)) for i in range(2)]
        ps_tr = [er(nc.psum_tensor(f"{tag}_pstr{i}", [16, 128], BF16)) for i in range(2)]
        s_pe = p.sem(tag)
        s_v = p.sem(tag)
        s_sc = p.sem(tag)
        NT = 32
        with nc.Block(no_gpsimd_drain=True) as blk:
            @blk.tensor
            def _(eng):
                for tt in range(NT):
                    if tt >= 2:
                        eng.wait_ge(s_v.h, s_v.base + (3 * (tt - 2)) + 2)
                    for k in range(G):
                        eng.matmul(ps_sc[tt % 2][:],
                                   XN[:, k * T + tt * 128:k * T + (tt + 1) * 128],
                                   gwT_sb[:, k * 8:(k + 1) * 8], start=(k == 0), stop=False)
                    i = eng.matmul(ps_sc[tt % 2][:], consts.ones_row_bf[:], gb_sb[:],
                                   start=False, stop=True)
                    s_pe.inc(i)
                    eng.wait_ge(s_v.h, s_v.base + (3 * tt + 2))
                    i = eng.transpose(ps_tr[tt % 2][:], rts[:, (tt % 2) * 16:(tt % 2) * 16 + 16],
                                      consts.ident[:])
                    s_pe.inc(i)

            @blk.vector
            def _(eng):
                for tt in range(NT):
                    sl = (tt % 2) * 16
                    eng.wait_ge(s_pe.h, s_pe.base + (2 * tt + 1))
                    eng.tensor_copy(sc[:, sl:sl + 8], ps_sc[tt % 2][:])
                    eng.drain()
                    eng.max(out=m8[:, 0:8], in_=sc[:, sl:sl + 8])
                    eng.drain()
                    i = eng.tensor_scalar_mul(m8[:, 8:9], m8[:, 0:1], -1.0)
                    i = eng.drain()
                    s_v.inc(i)
                    eng.wait_ge(s_sc.h, s_sc.base + (tt + 1))
                    eng.reduce_sum(m8[:, 9:10], e8[:, 0:8], axis=AX)
                    eng.drain()
                    eng.reciprocal(m8[:, 10:11], m8[:, 9:10])
                    eng.drain()
                    eng.tensor_scalar(sc[:, sl:sl + 8], e8[:, 0:8], m8[:, 10:11], None,
                                      ALU.mult)
                    eng.drain()
                    eng.max(out=m8[:, 0:8], in_=sc[:, sl:sl + 8])
                    eng.drain()
                    eng.tensor_scalar(e8[:, 8:16], sc[:, sl:sl + 8], m8[:, 1:2], None,
                                      ALU.is_ge)
                    eng.drain()
                    eng.tensor_mul(e8[:, 8:16], e8[:, 8:16], io8_sb[:])
                    eng.drain()
                    eng.reduce_max(m8[:, 11:12], e8[:, 8:16], axis=AX)
                    eng.drain()
                    eng.tensor_scalar(rts[:, sl:sl + 8], io8_sb[:], m8[:, 11:12], None,
                                      ALU.is_equal)
                    eng.drain()
                    eng.tensor_mul(e8[:, 8:16], sc[:, sl:sl + 8], rts[:, sl:sl + 8])
                    eng.drain()
                    eng.reduce_sum(m8[:, 12:13], e8[:, 8:16], axis=AX)
                    eng.drain()
                    eng.tensor_copy(rts[:, sl + 8:sl + 9], m8[:, 12:13])
                    eng.drain()
                    eng.memset(rts[:, sl + 9:sl + 16], 0.0)
                    i = eng.drain()
                    s_v.inc(i)
                    eng.wait_ge(s_pe.h, s_pe.base + (2 * tt + 2))
                    i = eng.tensor_copy(route[0:9, tt * 128:(tt + 1) * 128],
                                        ps_tr[tt % 2][0:9, :])
                    i = eng.drain()
                    s_v.inc(i)

            @blk.scalar
            def _(eng):
                for tt in range(NT):
                    sl = (tt % 2) * 16
                    eng.wait_ge(s_v.h, s_v.base + (3 * tt + 1))
                    i = eng.activation(e8[:, 0:8], sc[:, sl:sl + 8], AF.Exp,
                                       bias=m8[:, 8:9], scale=1.0)
                    eng.drain()
                    s_sc.inc(i)

        s_pe2 = p.sem(tag)
        s_v2 = p.sem(tag)
        ps_w = [er(nc.psum_tensor(f"{tag}_psw{i}", [128, TC], F32)) for i in range(2)]
        with nc.Block(no_gpsimd_drain=True) as blk:
            @blk.tensor
            def _(eng):
                for tc in range(NTC):
                    if tc >= 2:
                        eng.wait_ge(s_v2.h, s_v2.base + (tc - 1))
                    i = eng.matmul(ps_w[tc % 2][:], sel9_sb[:, 8 * 128:9 * 128],
                                   route[0:9, tc * TC:(tc + 1) * TC], start=True, stop=True)
                    s_pe2.inc(i)

            @blk.vector
            def _(eng):
                for tc in range(NTC):
                    eng.wait_ge(s_pe2.h, s_pe2.base + (tc + 1))
                    for g in range(G):
                        i = eng.tensor_mul(XN[:, g * T + tc * TC:g * T + (tc + 1) * TC],
                                           XN[:, g * T + tc * TC:g * T + (tc + 1) * TC],
                                           ps_w[tc % 2][:])
                    i = eng.drain()
                    s_v2.inc(i)

    if dbg is not None:
        dbg['route'] = route
        dbg['XN'] = XN
    if MOE1_TRUNC == 'route':
        _dummy_copy(p, ctx, consts, tag + '_dc', xT_in, xT_out)
        return
    TB = 1024
    NB = T // TB
    h_sb = e(nc.sbuf_tensor(f"{tag}_h", [128, 32 * TB], BF16))
    y_acc = e(nc.sbuf_tensor(f"{tag}_y", [128, G * TB], BF16))
    msk = e(nc.sbuf_tensor(f"{tag}_msk", [128, TB], BF16))
    w1b = e(nc.sbuf_tensor(f"{tag}_w1b", [128, 2 * G * 128], BF16))
    w2b = e(nc.sbuf_tensor(f"{tag}_w2b", [128, 2 * 32 * 128], BF16))

    for blkid in range(NB):
        tok0 = blkid * TB

        def xs(k, ts):
            return XN[:, k * T + tok0 + ts * TC: k * T + tok0 + (ts + 1) * TC]

        with ExitStack() as cB:
            eb = cB.enter_context
            p.stage_reset()
            ps_b2 = [eb(nc.psum_tensor(f"{tag}b{blkid}_psb{i}", [128, TC], F32))
                     for i in range(2)]
            ps_h = [eb(nc.psum_tensor(f"{tag}b{blkid}_ph{i}", [128, TC], F32))
                    for i in range(2)]
            ps_y = [eb(nc.psum_tensor(f"{tag}b{blkid}_py{i}", [128, TC], F32))
                    for i in range(2)]
            s_pe3 = p.sem(tag)
            s_sc3 = p.sem(tag)
            with nc.Block(no_gpsimd_drain=True) as blk:
                @blk.tensor
                def _(eng):
                    for j in range(2 * G):
                        d, ts = divmod(j, 2)
                        if j >= 2:
                            eng.wait_ge(s_sc3.h, s_sc3.base + (j - 1))
                        i = eng.matmul(ps_b2[j % 2][:], b2_sb[:, d * 128:(d + 1) * 128],
                                       route[0:8, tok0 + ts * TC:tok0 + (ts + 1) * TC],
                                       start=True, stop=True)
                        s_pe3.inc(i)

                @blk.scalar
                def _(eng):
                    for j in range(2 * G):
                        d, ts = divmod(j, 2)
                        eng.wait_ge(s_pe3.h, s_pe3.base + (j + 1))
                        i = eng.activation(y_acc[:, d * TB + ts * TC:d * TB + (ts + 1) * TC],
                                           ps_b2[j % 2][:], AF.Copy)
                        eng.drain()
                        s_sc3.inc(i)

            nE = E if not isinstance(MOE1_TRUNC, int) else MOE1_TRUNC
            p.stage_reset()
            s_wl = [p.sem(tag, 'sw'), p.sem(tag, 'sw')]
            s_w2l = [p.sem(tag, 'sw'), p.sem(tag, 'sw')]
            s_mm = p.sem(tag)
            s_ev = p.sem(tag)
            s_vv = p.sem(tag)
            s_pe4 = p.sem(tag)
            s_mk = p.sem(tag)
            with nc.Block(no_gpsimd_drain=True) as blk:
                @blk.tensor
                def _(eng):
                    for i_e in range(nE):
                        mmb = s_mm.base + 80 * i_e
                        evb = s_ev.base + 64 * i_e
                        vvb = s_vv.base + 80 * i_e
                        if i_e >= 1:
                            eng.wait_ge(s_vv.h, vvb)
                        for ts in range(2):
                            i = eng.matmul(ps_y[ts][:], sel9_sb[:, i_e * 128:(i_e + 1) * 128],
                                           route[0:9, tok0 + ts * TC:tok0 + (ts + 1) * TC],
                                           start=True, stop=True)
                        s_pe4.inc(i)
                        for m in range(32):
                            eng.wait_ge(s_wl[m % 2].h,
                                        s_wl[m % 2].base + (i_e * 16 + m // 2 + 1) * 16)
                            for ts in range(2):
                                j = m * 2 + ts
                                if j >= 2:
                                    eng.wait_ge(s_ev.h, evb + j - 1)
                                elif i_e >= 1:
                                    eng.wait_ge(s_ev.h, evb)
                                for k in range(G):
                                    i = eng.matmul(
                                        ps_h[j % 2][:],
                                        w1b[:, (m % 2) * G * 128 + k * 128:
                                            (m % 2) * G * 128 + (k + 1) * 128],
                                        xs(k, ts), start=(k == 0), stop=(k == G - 1))
                                s_mm.inc(i)
                        for d in range(G):
                            eng.wait_ge(s_w2l[d % 2].h,
                                        s_w2l[d % 2].base + (i_e * 4 + d // 2 + 1) * 16)
                            for ts in range(2):
                                j = d * 2 + ts
                                if j >= 2:
                                    eng.wait_ge(s_vv.h, vvb + 64 + j - 1)
                                for m in range(32):
                                    eng.wait_ge(s_vv.h, vvb + m * 2 + ts + 1)
                                    i = eng.matmul(
                                        ps_y[j % 2][:],
                                        w2b[:, (d % 2) * 32 * 128 + m * 128:
                                            (d % 2) * 32 * 128 + (m + 1) * 128],
                                        h_sb[:, m * TB + ts * TC:m * TB + (ts + 1) * TC],
                                        start=(m == 0), stop=(m == 31))
                                s_mm.inc(i)

                @blk.gpsimd
                def _(eng):
                    for i_e in range(nE):
                        mmb = s_mm.base + 80 * i_e
                        for m in range(32):
                            if m >= 2:
                                eng.wait_ge(s_mm.h, mmb + (m - 1) * 2)
                            elif i_e >= 1:
                                eng.wait_ge(s_mm.h, mmb)
                            i = eng.dma_start(
                                out=w1b[:, (m % 2) * G * 128:(m % 2 + 1) * G * 128],
                                in_=hap(w1T_h, (i_e * 32 + m) * 128 * 1024,
                                        [[1024, 128], [1, 1024]]))
                            s_wl[m % 2].inc(i, 16)
                        for d in range(G):
                            if d >= 2:
                                eng.wait_ge(s_mm.h, mmb + 64 + (d - 1) * 2)
                            elif i_e >= 1:
                                eng.wait_ge(s_mm.h, mmb)
                            i = eng.dma_start(
                                out=w2b[:, (d % 2) * 32 * 128:(d % 2 + 1) * 32 * 128],
                                in_=hap(w2T_h, (i_e * 8 + d) * 128 * 4096,
                                        [[4096, 128], [1, 4096]]))
                            s_w2l[d % 2].inc(i, 16)

                @blk.scalar
                def _(eng):
                    for i_e in range(nE):
                        mmb = s_mm.base + 80 * i_e
                        evb = s_ev.base + 64 * i_e
                        vvb = s_vv.base + 80 * i_e
                        eng.wait_ge(s_pe4.h, s_pe4.base + i_e + 1)
                        if i_e >= 1:
                            eng.wait_ge(s_vv.h, vvb - 16)
                        for ts in range(2):
                            i = eng.activation(msk[:, ts * TC:(ts + 1) * TC], ps_y[ts][:],
                                               AF.Copy)
                        i = eng.drain()
                        s_mk.inc(i)
                        for m in range(32):
                            for ts in range(2):
                                j = m * 2 + ts
                                eng.wait_ge(s_mm.h, mmb + j + 1)
                                i = eng.activation(
                                    h_sb[:, m * TB + ts * TC:m * TB + (ts + 1) * TC],
                                    ps_h[j % 2][:], AF.Relu,
                                    bias=b1_sb[:, i_e * 32 + m:i_e * 32 + m + 1])
                                eng.drain()
                                s_ev.inc(i)

                @blk.vector
                def _(eng):
                    for i_e in range(nE):
                        mmb = s_mm.base + 80 * i_e
                        evb = s_ev.base + 64 * i_e
                        eng.wait_ge(s_mk.h, s_mk.base + i_e + 1)
                        for m in range(32):
                            for ts in range(2):
                                j = m * 2 + ts
                                eng.wait_ge(s_ev.h, evb + j + 1)
                                i = eng.tensor_mul(
                                    h_sb[:, m * TB + ts * TC:m * TB + (ts + 1) * TC],
                                    h_sb[:, m * TB + ts * TC:m * TB + (ts + 1) * TC],
                                    msk[:, ts * TC:(ts + 1) * TC])
                                i = eng.drain()
                                s_vv.inc(i)
                        for jj in range(16):
                            eng.wait_ge(s_mm.h, mmb + 64 + jj + 1)
                            d, ts = divmod(jj, 2)
                            i = eng.scalar_tensor_tensor(
                                y_acc[:, d * TB + ts * TC:d * TB + (ts + 1) * TC],
                                ps_y[jj % 2][:], 1.0,
                                y_acc[:, d * TB + ts * TC:d * TB + (ts + 1) * TC],
                                ALU.mult, ALU.add)
                            i = eng.drain()
                            s_vv.inc(i)

        with ExitStack() as cB2:
            eb2 = cB2.enter_context
            p.stage_reset()
            xld = eb2(nc.sbuf_tensor(f"{tag}b{blkid}_xld", [128, 2 * TC], F32))
            stg = eb2(nc.sbuf_tensor(f"{tag}b{blkid}_stg", [128, 2 * TC], F32))
            s_xl = [p.sem(tag, 'hw'), p.sem(tag, 'hw')]
            s_st = [p.sem(tag, 'hw'), p.sem(tag, 'hw')]
            s_ev2 = p.sem(tag)
            with nc.Block(no_gpsimd_drain=True) as blk:
                @blk.sync
                def _(eng):
                    for j in range(2 * G):
                        d, ts = divmod(j, 2)
                        if j >= 2:
                            eng.wait_ge(s_ev2.h, s_ev2.base + (j - 1))
                        i = eng.dma_start(
                            out=xld[:, (j % 2) * TC:(j % 2 + 1) * TC],
                            in_=xT_in[d * 128:(d + 1) * 128,
                                      tok0 + ts * TC:tok0 + (ts + 1) * TC])
                        s_xl[j % 2].inc(i, 16)
                        eng.wait_ge(s_ev2.h, s_ev2.base + (j + 1))
                        i = eng.dma_start(
                            out=xT_out[d * 128:(d + 1) * 128,
                                       tok0 + ts * TC:tok0 + (ts + 1) * TC],
                            in_=stg[:, (j % 2) * TC:(j % 2 + 1) * TC])
                        s_st[j % 2].inc(i, 16)
                    eng.wait_ge(s_st[0].h, s_st[0].v)
                    eng.wait_ge(s_st[1].h, s_st[1].v)

                @blk.vector
                def _(eng):
                    for j in range(2 * G):
                        d, ts = divmod(j, 2)
                        eng.wait_ge(s_xl[j % 2].h, s_xl[j % 2].base + ((j // 2 + 1)) * 16)
                        if j >= 2:
                            eng.wait_ge(s_st[j % 2].h, s_st[j % 2].base + ((j // 2)) * 16)
                        i = eng.tensor_add(stg[:, (j % 2) * TC:(j % 2 + 1) * TC],
                                           xld[:, (j % 2) * TC:(j % 2 + 1) * TC],
                                           y_acc[:, d * TB + ts * TC:d * TB + (ts + 1) * TC])
                        i = eng.drain()
                        s_ev2.inc(i)


def emit_moe2(p, ctx, consts, tag, xT_in, xT_out, W, wname):
    nc = p.nc
    E, H = 32, 512
    Wg_h = p.inp(f'{wname}_Wg', np.ascontiguousarray(
        W['Wg'].reshape(8, 128, 32).transpose(1, 0, 2).reshape(128, 8 * 32)))
    W1_h = p.inp(f'{wname}_W1', W['W1'])
    W3_h = p.inp(f'{wname}_W3', W['W3'])
    W2_h = p.inp(f'{wname}_W2', W['W2'])
    Ws1_h = p.inp(f'{wname}_Ws1', W['Ws1'])
    Ws3_h = p.inp(f'{wname}_Ws3', W['Ws3'])
    Ws2_h = p.inp(f'{wname}_Ws2', W['Ws2'])
    io32_h = p.inp('c_iota32', np.broadcast_to(np.arange(32, dtype=np.float32),
                                               (128, 32)).copy())
    sel32_h = p.inp('c_sel32', np.kron(np.eye(32, dtype=np.float32),
                                       np.ones((1, 128), np.float32)))

    e = ctx.enter_context
    XN = e(nc.sbuf_tensor(f"{tag}_xn", [128, G * T], BF16))
    with ExitStack() as cL:
        emit_ln(p, cL, consts, f"{tag}_ln", xT_in, XN)
    (gw_sb, io32_sb, sel32_sb) = load_small_sbuf(
        p, ctx, f"{tag}_sc", [(Wg_h, (128, G * 32), BF16), (io32_h, (128, 32), F32),
                              (sel32_h, (32, 32 * 128), BF16)])
    route = e(nc.sbuf_tensor(f"{tag}_rt", [32, T], BF16))

    with ExitStack() as cR:
        er = cR.enter_context
        p.stage_reset()
        sc = er(nc.sbuf_tensor(f"{tag}_scb", [128, 2 * 32], F32))
        e32 = er(nc.sbuf_tensor(f"{tag}_e32", [128, 32], F32))
        m8 = er(nc.sbuf_tensor(f"{tag}_m8", [128, 16], F32))
        rts = er(nc.sbuf_tensor(f"{tag}_rts", [128, 2 * 32], BF16))
        ps_sc = [er(nc.psum_tensor(f"{tag}_pssc{i}", [128, 32], F32)) for i in range(2)]
        ps_tr = [er(nc.psum_tensor(f"{tag}_pstr{i}", [32, 128], BF16)) for i in range(2)]
        s_pe = p.sem(tag)
        s_v = p.sem(tag)
        s_sc = p.sem(tag)
        NT = 32
        with nc.Block(no_gpsimd_drain=True) as blk:
            @blk.tensor
            def _(eng):
                for tt in range(NT):
                    if tt >= 2:
                        eng.wait_ge(s_v.h, s_v.base + (3 * (tt - 2)) + 2)
                    for k in range(G):
                        i = eng.matmul(ps_sc[tt % 2][:],
                                       XN[:, k * T + tt * 128:k * T + (tt + 1) * 128],
                                       gw_sb[:, k * 32:(k + 1) * 32],
                                       start=(k == 0), stop=(k == G - 1))
                    s_pe.inc(i)
                    eng.wait_ge(s_v.h, s_v.base + (3 * tt + 2))
                    i = eng.transpose(ps_tr[tt % 2][:], rts[:, (tt % 2) * 32:(tt % 2 + 1) * 32],
                                      consts.ident[:])
                    s_pe.inc(i)

            @blk.vector
            def _(eng):
                for tt in range(NT):
                    sl = (tt % 2) * 32
                    eng.wait_ge(s_pe.h, s_pe.base + (2 * tt + 1))
                    eng.tensor_copy(sc[:, sl:sl + 32], ps_sc[tt % 2][:])
                    eng.drain()
                    eng.max(out=m8[:, 0:8], in_=sc[:, sl:sl + 32])
                    eng.drain()
                    i = eng.tensor_scalar_mul(m8[:, 8:9], m8[:, 0:1], -1.0)
                    i = eng.drain()
                    s_v.inc(i)
                    eng.wait_ge(s_sc.h, s_sc.base + (tt + 1))
                    eng.reduce_sum(m8[:, 9:10], e32[:, 0:32], axis=AX)
                    eng.drain()
                    eng.reciprocal(m8[:, 10:11], m8[:, 9:10])
                    eng.drain()
                    eng.tensor_scalar(sc[:, sl:sl + 32], e32[:, 0:32], m8[:, 10:11], None,
                                      ALU.mult)
                    eng.drain()
                    eng.max(out=m8[:, 0:8], in_=sc[:, sl:sl + 32])
                    eng.drain()
                    eng.tensor_scalar(e32[:, 0:32], sc[:, sl:sl + 32], m8[:, 3:4], None,
                                      ALU.is_ge)
                    eng.drain()
                    i = eng.tensor_mul(rts[:, sl:sl + 32], sc[:, sl:sl + 32], e32[:, 0:32])
                    i = eng.drain()
                    s_v.inc(i)
                    eng.wait_ge(s_pe.h, s_pe.base + (2 * tt + 2))
                    i = eng.tensor_copy(route[0:32, tt * 128:(tt + 1) * 128],
                                        ps_tr[tt % 2][0:32, :])
                    i = eng.drain()
                    s_v.inc(i)

            @blk.scalar
            def _(eng):
                for tt in range(NT):
                    sl = (tt % 2) * 32
                    eng.wait_ge(s_v.h, s_v.base + (3 * tt + 1))
                    i = eng.activation(e32[:, 0:32], sc[:, sl:sl + 32], AF.Exp,
                                       bias=m8[:, 8:9], scale=1.0)
                    eng.drain()
                    s_sc.inc(i)

    TB = 1024
    NB = T // TB
    y_acc = e(nc.sbuf_tensor(f"{tag}_y", [128, G * TB], F32))
    g_sb = e(nc.sbuf_tensor(f"{tag}_g", [128, 8 * TC], BF16))
    gs_sb = e(nc.sbuf_tensor(f"{tag}_gs", [128, 16 * TC], BF16))
    a_sb = e(nc.sbuf_tensor(f"{tag}_a", [128, 2 * TC], BF16))
    wmsk = e(nc.sbuf_tensor(f"{tag}_wm", [128, 2 * TC], BF16))
    w1b = e(nc.sbuf_tensor(f"{tag}_w1b", [128, 2 * 32 * 128], BF16))
    w3b = e(nc.sbuf_tensor(f"{tag}_w3b", [128, 2 * 32 * 128], BF16))
    w2b = e(nc.sbuf_tensor(f"{tag}_w2b", [128, 2 * 32 * 128], BF16))

    for blkid in range(NB):
        tok0 = blkid * TB

        def xs(k, ts):
            return XN[:, k * T + tok0 + ts * TC: k * T + tok0 + (ts + 1) * TC]

        with ExitStack() as cS:
            es = cS.enter_context
            p.stage_reset()
            ps_a = [es(nc.psum_tensor(f"{tag}b{blkid}_pa{i}", [128, TC], F32)) for i in range(2)]
            ps_b = [es(nc.psum_tensor(f"{tag}b{blkid}_pb{i}", [128, TC], F32)) for i in range(2)]
            ps_y = [es(nc.psum_tensor(f"{tag}b{blkid}_pyy{i}", [128, TC], F32)) for i in range(2)]
            s_wl = [p.sem(tag, 'sw'), p.sem(tag, 'sw')]
            s_mm = p.sem(tag)
            s_ev = p.sem(tag)
            s_vv = p.sem(tag)
            ws1b = es(nc.sbuf_tensor(f"{tag}b{blkid}_ws1b", [128, 2 * G * 128], BF16))
            ws3b = es(nc.sbuf_tensor(f"{tag}b{blkid}_ws3b", [128, 2 * G * 128], BF16))
            with nc.Block(no_gpsimd_drain=True) as blk:
                @blk.gpsimd
                def _(eng):
                    for hc in range(8):
                        if hc >= 2:
                            eng.wait_ge(s_mm.h, s_mm.base + ((hc - 1)) * 4)
                        i = eng.dma_start(
                            out=ws1b[:, (hc % 2) * G * 128:(hc % 2 + 1) * G * 128].rearrange(
                                "p (k j) -> p k j", j=128),
                            in_=hap(Ws1_h, hc * 128, [[1024, 128], [128 * 1024, G], [1, 128]]))
                        s_wl[hc % 2].inc(i, 16)
                        i = eng.dma_start(
                            out=ws3b[:, (hc % 2) * G * 128:(hc % 2 + 1) * G * 128].rearrange(
                                "p (k j) -> p k j", j=128),
                            in_=hap(Ws3_h, hc * 128, [[1024, 128], [128 * 1024, G], [1, 128]]))
                        s_wl[hc % 2].inc(i, 16)

                @blk.tensor
                def _(eng):
                    for hc in range(8):
                        eng.wait_ge(s_wl[hc % 2].h, s_wl[hc % 2].base + ((hc // 2 + 1)) * 32)
                        for ts in range(2):
                            j = hc * 2 + ts
                            if j >= 2:
                                eng.wait_ge(s_ev.h, s_ev.base + (j - 1))
                            for k in range(G):
                                i = eng.matmul(ps_a[j % 2][:],
                                               ws1b[:, (hc % 2) * G * 128 + k * 128:
                                                    (hc % 2) * G * 128 + (k + 1) * 128],
                                               xs(k, ts), start=(k == 0), stop=(k == G - 1))
                            s_mm.inc(i)
                            if j >= 2:
                                eng.wait_ge(s_vv.h, s_vv.base + (j - 1))
                            for k in range(G):
                                i = eng.matmul(ps_b[j % 2][:],
                                               ws3b[:, (hc % 2) * G * 128 + k * 128:
                                                    (hc % 2) * G * 128 + (k + 1) * 128],
                                               xs(k, ts), start=(k == 0), stop=(k == G - 1))
                            s_mm.inc(i)

                @blk.scalar
                def _(eng):
                    for j in range(16):
                        eng.wait_ge(s_mm.h, s_mm.base + (2 * j + 1))
                        if j >= 2:
                            eng.wait_ge(s_vv.h, s_vv.base + (j - 1))
                        i = eng.activation(a_sb[:, (j % 2) * TC:(j % 2 + 1) * TC],
                                           ps_a[j % 2][:], AF.Silu)
                        eng.drain()
                        s_ev.inc(i)

                @blk.vector
                def _(eng):
                    for j in range(16):
                        eng.wait_ge(s_mm.h, s_mm.base + (2 * j + 2))
                        eng.wait_ge(s_ev.h, s_ev.base + (j + 1))
                        i = eng.tensor_mul(gs_sb[:, j * TC:(j + 1) * TC],
                                           a_sb[:, (j % 2) * TC:(j % 2 + 1) * TC],
                                           ps_b[j % 2][:])
                        i = eng.drain()
                        s_vv.inc(i)

            p.stage_reset()
            s_wl2 = [p.sem(tag, 'sw'), p.sem(tag, 'sw')]
            s_mm2 = p.sem(tag)
            s_sc2 = p.sem(tag)
            with nc.Block(no_gpsimd_drain=True) as blk:
                @blk.gpsimd
                def _(eng):
                    for d in range(G):
                        if d >= 2:
                            eng.wait_ge(s_mm2.h, s_mm2.base + ((d - 1)) * 2)
                        i = eng.dma_start(
                            out=w2b[:, (d % 2) * 16 * 128:(d % 2) * 16 * 128 + 8 * 128].rearrange(
                                "p (k j) -> p k j", j=128),
                            in_=hap(Ws2_h, d * 128, [[1024, 128], [128 * 1024, G], [1, 128]]))
                        s_wl2[d % 2].inc(i, 16)

                @blk.tensor
                def _(eng):
                    for d in range(G):
                        eng.wait_ge(s_wl2[d % 2].h, s_wl2[d % 2].base + ((d // 2 + 1)) * 16)
                        for ts in range(2):
                            j = d * 2 + ts
                            if j >= 2:
                                eng.wait_ge(s_sc2.h, s_sc2.base + (j - 1))
                            for k in range(G):
                                i = eng.matmul(ps_y[j % 2][:],
                                               w2b[:, (d % 2) * 16 * 128 + k * 128:
                                                    (d % 2) * 16 * 128 + (k + 1) * 128],
                                               gs_sb[:, (k * 2 + ts) * TC:(k * 2 + ts + 1) * TC],
                                               start=(k == 0), stop=(k == G - 1))
                            s_mm2.inc(i)

                @blk.scalar
                def _(eng):
                    for d in range(G):
                        for ts in range(2):
                            j = d * 2 + ts
                            eng.wait_ge(s_mm2.h, s_mm2.base + (j + 1))
                            i = eng.activation(
                                y_acc[:, d * TB + ts * TC:d * TB + (ts + 1) * TC],
                                ps_y[j % 2][:], AF.Copy)
                            eng.drain()
                            s_sc2.inc(i)

            p.stage_reset()
            s_wl3 = [p.sem(tag, 'sw'), p.sem(tag, 'sw')]
            s_mm3 = p.sem(tag)
            s_ev3 = p.sem(tag)
            s_vv3 = p.sem(tag)
            s_mk3 = p.sem(tag)
            s_pe5 = p.sem(tag)
            WSL = 32 * 128
            with nc.Block(no_gpsimd_drain=True) as blk:
                @blk.gpsimd
                def _(eng):
                    for i_e in range(E):
                        sl = (i_e % 2) * WSL
                        if i_e >= 2:
                            eng.wait_ge(s_mm3.h, s_mm3.base + 32 * (i_e - 1))
                        for hc in range(4):
                            i = eng.dma_start(
                                out=w1b[:, sl + hc * G * 128:sl + (hc + 1) * G * 128].rearrange(
                                    "p (k j) -> p k j", j=128),
                                in_=hap(W1_h, i_e * 1024 * H + hc * 128,
                                        [[H, 128], [128 * H, G], [1, 128]]))
                            s_wl3[i_e % 2].inc(i, 16)
                        for hc in range(4):
                            i = eng.dma_start(
                                out=w3b[:, sl + hc * G * 128:sl + (hc + 1) * G * 128].rearrange(
                                    "p (k j) -> p k j", j=128),
                                in_=hap(W3_h, i_e * 1024 * H + hc * 128,
                                        [[H, 128], [128 * H, G], [1, 128]]))
                            s_wl3[i_e % 2].inc(i, 16)
                        for d in range(G):
                            i = eng.dma_start(
                                out=w2b[:, sl + d * 4 * 128:sl + (d + 1) * 4 * 128].rearrange(
                                    "p (k j) -> p k j", j=128),
                                in_=hap(W2_h, i_e * H * 1024 + d * 128,
                                        [[1024, 128], [128 * 1024, 4], [1, 128]]))
                            s_wl3[i_e % 2].inc(i, 16)

                @blk.tensor
                def _(eng):
                    for i_e in range(E):
                        sl = (i_e % 2) * WSL
                        mmb = s_mm3.base + 32 * i_e
                        evb = s_ev3.base + 8 * i_e
                        vvb = s_vv3.base + 24 * i_e
                        eng.wait_ge(s_vv3.h, vvb)
                        for ts in range(2):
                            i = eng.matmul(ps_y[ts][:], sel32_sb[:, i_e * 128:(i_e + 1) * 128],
                                           route[0:32, tok0 + ts * TC:tok0 + (ts + 1) * TC],
                                           start=True, stop=True)
                        s_pe5.inc(i)
                        eng.wait_ge(s_wl3[i_e % 2].h,
                                    s_wl3[i_e % 2].base + (i_e // 2 + 1) * 256)
                        for hc in range(4):
                            for ts in range(2):
                                j = hc * 2 + ts
                                if j >= 2:
                                    eng.wait_ge(s_ev3.h, evb + j - 1)
                                elif i_e >= 1:
                                    eng.wait_ge(s_ev3.h, evb)
                                for k in range(G):
                                    i = eng.matmul(ps_a[j % 2][:],
                                                   w1b[:, sl + hc * G * 128 + k * 128:
                                                        sl + hc * G * 128 + (k + 1) * 128],
                                                   xs(k, ts), start=(k == 0), stop=(k == G - 1))
                                s_mm3.inc(i)
                                if j >= 2:
                                    eng.wait_ge(s_vv3.h, vvb + j - 1)
                                elif i_e >= 1:
                                    eng.wait_ge(s_vv3.h, vvb - 16 + 8)
                                for k in range(G):
                                    i = eng.matmul(ps_b[j % 2][:],
                                                   w3b[:, sl + hc * G * 128 + k * 128:
                                                        sl + hc * G * 128 + (k + 1) * 128],
                                                   xs(k, ts), start=(k == 0), stop=(k == G - 1))
                                s_mm3.inc(i)
                        for d in range(G):
                            for ts in range(2):
                                jj = d * 2 + ts
                                eng.wait_ge(s_vv3.h, vvb + 8)
                                if jj >= 2:
                                    eng.wait_ge(s_vv3.h, vvb + 8 + jj - 1)
                                for hc in range(4):
                                    i = eng.matmul(ps_y[jj % 2][:],
                                                   w2b[:, sl + d * 4 * 128 + hc * 128:
                                                        sl + d * 4 * 128 + (hc + 1) * 128],
                                                   g_sb[:, (hc * 2 + ts) * TC:
                                                        (hc * 2 + ts + 1) * TC],
                                                   start=(hc == 0), stop=(hc == 3))
                                s_mm3.inc(i)

                @blk.scalar
                def _(eng):
                    for i_e in range(E):
                        mmb = s_mm3.base + 32 * i_e
                        evb = s_ev3.base + 8 * i_e
                        vvb = s_vv3.base + 24 * i_e
                        eng.wait_ge(s_pe5.h, s_pe5.base + i_e + 1)
                        if i_e >= 1:
                            eng.wait_ge(s_vv3.h, vvb - 16)
                        for ts in range(2):
                            i = eng.activation(wmsk[:, ts * TC:(ts + 1) * TC], ps_y[ts][:],
                                               AF.Copy)
                        i = eng.drain()
                        s_mk3.inc(i)
                        for j in range(8):
                            eng.wait_ge(s_mm3.h, mmb + 2 * j + 1)
                            if j >= 2:
                                eng.wait_ge(s_vv3.h, vvb + j - 1)
                            elif i_e >= 1:
                                eng.wait_ge(s_vv3.h, vvb - 16 + 8)
                            i = eng.activation(a_sb[:, (j % 2) * TC:(j % 2 + 1) * TC],
                                               ps_a[j % 2][:], AF.Silu)
                            eng.drain()
                            s_ev3.inc(i)

                @blk.vector
                def _(eng):
                    for i_e in range(E):
                        mmb = s_mm3.base + 32 * i_e
                        evb = s_ev3.base + 8 * i_e
                        eng.wait_ge(s_mk3.h, s_mk3.base + i_e + 1)
                        for j in range(8):
                            eng.wait_ge(s_mm3.h, mmb + 2 * j + 2)
                            eng.wait_ge(s_ev3.h, evb + j + 1)
                            ts = j % 2
                            eng.tensor_mul(g_sb[:, j * TC:(j + 1) * TC],
                                           a_sb[:, (j % 2) * TC:(j % 2 + 1) * TC],
                                           ps_b[j % 2][:])
                            eng.drain()
                            i = eng.tensor_mul(g_sb[:, j * TC:(j + 1) * TC],
                                               g_sb[:, j * TC:(j + 1) * TC],
                                               wmsk[:, ts * TC:(ts + 1) * TC])
                            i = eng.drain()
                            s_vv3.inc(i)
                        for jj in range(16):
                            eng.wait_ge(s_mm3.h, mmb + 16 + jj + 1)
                            d, ts = divmod(jj, 2)
                            i = eng.scalar_tensor_tensor(
                                y_acc[:, d * TB + ts * TC:d * TB + (ts + 1) * TC],
                                ps_y[jj % 2][:], 1.0,
                                y_acc[:, d * TB + ts * TC:d * TB + (ts + 1) * TC],
                                ALU.mult, ALU.add)
                            i = eng.drain()
                            s_vv3.inc(i)

        with ExitStack() as cB2:
            eb2 = cB2.enter_context
            p.stage_reset()
            xld = eb2(nc.sbuf_tensor(f"{tag}b{blkid}_xld", [128, 2 * TC], F32))
            stg = eb2(nc.sbuf_tensor(f"{tag}b{blkid}_stg", [128, 2 * TC], F32))
            s_xl = [p.sem(tag, 'hw'), p.sem(tag, 'hw')]
            s_st = [p.sem(tag, 'hw'), p.sem(tag, 'hw')]
            s_ev2 = p.sem(tag)
            with nc.Block(no_gpsimd_drain=True) as blk:
                @blk.sync
                def _(eng):
                    for j in range(2 * G):
                        d, ts = divmod(j, 2)
                        if j >= 2:
                            eng.wait_ge(s_ev2.h, s_ev2.base + (j - 1))
                        i = eng.dma_start(
                            out=xld[:, (j % 2) * TC:(j % 2 + 1) * TC],
                            in_=xT_in[d * 128:(d + 1) * 128,
                                      tok0 + ts * TC:tok0 + (ts + 1) * TC])
                        s_xl[j % 2].inc(i, 16)
                        eng.wait_ge(s_ev2.h, s_ev2.base + (j + 1))
                        i = eng.dma_start(
                            out=xT_out[d * 128:(d + 1) * 128,
                                       tok0 + ts * TC:tok0 + (ts + 1) * TC],
                            in_=stg[:, (j % 2) * TC:(j % 2 + 1) * TC])
                        s_st[j % 2].inc(i, 16)
                    eng.wait_ge(s_st[0].h, s_st[0].v)
                    eng.wait_ge(s_st[1].h, s_st[1].v)

                @blk.vector
                def _(eng):
                    for j in range(2 * G):
                        d, ts = divmod(j, 2)
                        eng.wait_ge(s_xl[j % 2].h, s_xl[j % 2].base + ((j // 2 + 1)) * 16)
                        if j >= 2:
                            eng.wait_ge(s_st[j % 2].h, s_st[j % 2].base + ((j // 2)) * 16)
                        i = eng.tensor_add(stg[:, (j % 2) * TC:(j % 2 + 1) * TC],
                                           xld[:, (j % 2) * TC:(j % 2 + 1) * TC],
                                           y_acc[:, d * TB + ts * TC:d * TB + (ts + 1) * TC])
                        i = eng.drain()
                        s_ev2.inc(i)


def emit_moe2(p, ctx, consts, tag, xT_in, xT_out, W, wname):
    nc = p.nc
    E, H = 32, 512
    Wg_h = p.inp(f'{wname}_Wg', np.ascontiguousarray(
        W['Wg'].reshape(8, 128, 32).transpose(1, 0, 2).reshape(128, 8 * 32)))
    W1_h = p.inp(f'{wname}_W1', W['W1'])
    W3_h = p.inp(f'{wname}_W3', W['W3'])
    W2_h = p.inp(f'{wname}_W2', W['W2'])
    Ws1_h = p.inp(f'{wname}_Ws1', W['Ws1'])
    Ws3_h = p.inp(f'{wname}_Ws3', W['Ws3'])
    Ws2_h = p.inp(f'{wname}_Ws2', W['Ws2'])
    io32_h = p.inp('c_iota32', np.broadcast_to(np.arange(32, dtype=np.float32),
                                               (128, 32)).copy())
    sel32_h = p.inp('c_sel32', np.kron(np.eye(32, dtype=np.float32),
                                       np.ones((1, 128), np.float32)))

    e = ctx.enter_context
    XN = e(nc.sbuf_tensor(f"{tag}_xn", [128, G * T], BF16))
    with ExitStack() as cL:
        emit_ln(p, cL, consts, f"{tag}_ln", xT_in, XN)
    (gw_sb, io32_sb, sel32_sb) = load_small_sbuf(
        p, ctx, f"{tag}_sc", [(Wg_h, (128, G * 32), BF16), (io32_h, (128, 32), F32),
                              (sel32_h, (32, 32 * 128), BF16)])
    route = e(nc.sbuf_tensor(f"{tag}_rt", [32, T], BF16))

    with ExitStack() as cR:
        er = cR.enter_context
        p.stage_reset()
        sc = er(nc.sbuf_tensor(f"{tag}_scb", [128, 2 * 32], F32))
        e32 = er(nc.sbuf_tensor(f"{tag}_e32", [128, 32], F32))
        m8 = er(nc.sbuf_tensor(f"{tag}_m8", [128, 16], F32))
        rts = er(nc.sbuf_tensor(f"{tag}_rts", [128, 2 * 32], BF16))
        ps_sc = [er(nc.psum_tensor(f"{tag}_pssc{i}", [128, 32], F32)) for i in range(2)]
        ps_tr = [er(nc.psum_tensor(f"{tag}_pstr{i}", [32, 128], BF16)) for i in range(2)]
        s_pe = p.sem(tag)
        s_v = p.sem(tag)
        s_sc = p.sem(tag)
        NT = 32
        with nc.Block(no_gpsimd_drain=True) as blk:
            @blk.tensor
            def _(eng):
                for tt in range(NT):
                    if tt >= 2:
                        eng.wait_ge(s_v.h, s_v.base + (3 * (tt - 2)) + 2)
                    for k in range(G):
                        i = eng.matmul(ps_sc[tt % 2][:],
                                       XN[:, k * T + tt * 128:k * T + (tt + 1) * 128],
                                       gw_sb[:, k * 32:(k + 1) * 32],
                                       start=(k == 0), stop=(k == G - 1))
                    s_pe.inc(i)
                    eng.wait_ge(s_v.h, s_v.base + (3 * tt + 2))
                    i = eng.transpose(ps_tr[tt % 2][:], rts[:, (tt % 2) * 32:(tt % 2 + 1) * 32],
                                      consts.ident[:])
                    s_pe.inc(i)

            @blk.vector
            def _(eng):
                for tt in range(NT):
                    sl = (tt % 2) * 32
                    eng.wait_ge(s_pe.h, s_pe.base + (2 * tt + 1))
                    eng.tensor_copy(sc[:, sl:sl + 32], ps_sc[tt % 2][:])
                    eng.drain()
                    eng.max(out=m8[:, 0:8], in_=sc[:, sl:sl + 32])
                    eng.drain()
                    i = eng.tensor_scalar_mul(m8[:, 8:9], m8[:, 0:1], -1.0)
                    i = eng.drain()
                    s_v.inc(i)
                    eng.wait_ge(s_sc.h, s_sc.base + (tt + 1))
                    eng.reduce_sum(m8[:, 9:10], e32[:, 0:32], axis=AX)
                    eng.drain()
                    eng.reciprocal(m8[:, 10:11], m8[:, 9:10])
                    eng.drain()
                    eng.tensor_scalar(sc[:, sl:sl + 32], e32[:, 0:32], m8[:, 10:11], None,
                                      ALU.mult)
                    eng.drain()
                    eng.max(out=m8[:, 0:8], in_=sc[:, sl:sl + 32])
                    eng.drain()
                    eng.tensor_scalar(e32[:, 0:32], sc[:, sl:sl + 32], m8[:, 3:4], None,
                                      ALU.is_ge)
                    eng.drain()
                    i = eng.tensor_mul(rts[:, sl:sl + 32], sc[:, sl:sl + 32], e32[:, 0:32])
                    i = eng.drain()
                    s_v.inc(i)
                    eng.wait_ge(s_pe.h, s_pe.base + (2 * tt + 2))
                    i = eng.tensor_copy(route[0:32, tt * 128:(tt + 1) * 128],
                                        ps_tr[tt % 2][0:32, :])
                    i = eng.drain()
                    s_v.inc(i)

            @blk.scalar
            def _(eng):
                for tt in range(NT):
                    sl = (tt % 2) * 32
                    eng.wait_ge(s_v.h, s_v.base + (3 * tt + 1))
                    i = eng.activation(e32[:, 0:32], sc[:, sl:sl + 32], AF.Exp,
                                       bias=m8[:, 8:9], scale=1.0)
                    eng.drain()
                    s_sc.inc(i)

    TB = 1024
    NB = T // TB
    y_acc = e(nc.sbuf_tensor(f"{tag}_y", [128, G * TB], F32))
    g_sb = e(nc.sbuf_tensor(f"{tag}_g", [128, 8 * TC], BF16))
    gs_sb = e(nc.sbuf_tensor(f"{tag}_gs", [128, 16 * TC], BF16))
    a_sb = e(nc.sbuf_tensor(f"{tag}_a", [128, 2 * TC], BF16))
    wmsk = e(nc.sbuf_tensor(f"{tag}_wm", [128, 2 * TC], BF16))
    w1b = e(nc.sbuf_tensor(f"{tag}_w1b", [128, 2 * 32 * 128], BF16))
    w3b = e(nc.sbuf_tensor(f"{tag}_w3b", [128, 2 * 32 * 128], BF16))
    w2b = e(nc.sbuf_tensor(f"{tag}_w2b", [128, 2 * 32 * 128], BF16))

    for blkid in range(NB):
        tok0 = blkid * TB

        def xs(k, ts):
            return XN[:, k * T + tok0 + ts * TC: k * T + tok0 + (ts + 1) * TC]

        with ExitStack() as cS:
            es = cS.enter_context
            p.stage_reset()
            ps_a = [es(nc.psum_tensor(f"{tag}b{blkid}_pa{i}", [128, TC], F32)) for i in range(2)]
            ps_b = [es(nc.psum_tensor(f"{tag}b{blkid}_pb{i}", [128, TC], F32)) for i in range(2)]
            ps_y = [es(nc.psum_tensor(f"{tag}b{blkid}_pyy{i}", [128, TC], F32)) for i in range(2)]
            s_wl = [p.sem(tag, 'sw'), p.sem(tag, 'sw')]
            s_mm = p.sem(tag)
            s_ev = p.sem(tag)
            s_vv = p.sem(tag)
            ws1b = es(nc.sbuf_tensor(f"{tag}b{blkid}_ws1b", [128, 2 * G * 128], BF16))
            ws3b = es(nc.sbuf_tensor(f"{tag}b{blkid}_ws3b", [128, 2 * G * 128], BF16))
            with nc.Block(no_gpsimd_drain=True) as blk:
                @blk.gpsimd
                def _(eng):
                    for hc in range(8):
                        if hc >= 2:
                            eng.wait_ge(s_mm.h, s_mm.base + ((hc - 1)) * 4)
                        i = eng.dma_start(
                            out=ws1b[:, (hc % 2) * G * 128:(hc % 2 + 1) * G * 128].rearrange(
                                "p (k j) -> p k j", j=128),
                            in_=hap(Ws1_h, hc * 128, [[1024, 128], [128 * 1024, G], [1, 128]]))
                        s_wl[hc % 2].inc(i, 16)
                        i = eng.dma_start(
                            out=ws3b[:, (hc % 2) * G * 128:(hc % 2 + 1) * G * 128].rearrange(
                                "p (k j) -> p k j", j=128),
                            in_=hap(Ws3_h, hc * 128, [[1024, 128], [128 * 1024, G], [1, 128]]))
                        s_wl[hc % 2].inc(i, 16)

                @blk.tensor
                def _(eng):
                    for hc in range(8):
                        eng.wait_ge(s_wl[hc % 2].h, s_wl[hc % 2].base + ((hc // 2 + 1)) * 32)
                        for ts in range(2):
                            j = hc * 2 + ts
                            if j >= 2:
                                eng.wait_ge(s_ev.h, s_ev.base + (j - 1))
                            for k in range(G):
                                i = eng.matmul(ps_a[j % 2][:],
                                               ws1b[:, (hc % 2) * G * 128 + k * 128:
                                                    (hc % 2) * G * 128 + (k + 1) * 128],
                                               xs(k, ts), start=(k == 0), stop=(k == G - 1))
                            s_mm.inc(i)
                            if j >= 2:
                                eng.wait_ge(s_vv.h, s_vv.base + (j - 1))
                            for k in range(G):
                                i = eng.matmul(ps_b[j % 2][:],
                                               ws3b[:, (hc % 2) * G * 128 + k * 128:
                                                    (hc % 2) * G * 128 + (k + 1) * 128],
                                               xs(k, ts), start=(k == 0), stop=(k == G - 1))
                            s_mm.inc(i)

                @blk.scalar
                def _(eng):
                    for j in range(16):
                        eng.wait_ge(s_mm.h, s_mm.base + (2 * j + 1))
                        if j >= 2:
                            eng.wait_ge(s_vv.h, s_vv.base + (j - 1))
                        i = eng.activation(a_sb[:, (j % 2) * TC:(j % 2 + 1) * TC],
                                           ps_a[j % 2][:], AF.Silu)
                        eng.drain()
                        s_ev.inc(i)

                @blk.vector
                def _(eng):
                    for j in range(16):
                        eng.wait_ge(s_mm.h, s_mm.base + (2 * j + 2))
                        eng.wait_ge(s_ev.h, s_ev.base + (j + 1))
                        i = eng.tensor_mul(gs_sb[:, j * TC:(j + 1) * TC],
                                           a_sb[:, (j % 2) * TC:(j % 2 + 1) * TC],
                                           ps_b[j % 2][:])
                        i = eng.drain()
                        s_vv.inc(i)

            p.stage_reset()
            s_wl2 = [p.sem(tag, 'sw'), p.sem(tag, 'sw')]
            s_mm2 = p.sem(tag)
            s_sc2 = p.sem(tag)
            with nc.Block(no_gpsimd_drain=True) as blk:
                @blk.gpsimd
                def _(eng):
                    for d in range(G):
                        if d >= 2:
                            eng.wait_ge(s_mm2.h, s_mm2.base + ((d - 1)) * 2)
                        i = eng.dma_start(
                            out=w2b[:, (d % 2) * 16 * 128:(d % 2) * 16 * 128 + 8 * 128].rearrange(
                                "p (k j) -> p k j", j=128),
                            in_=hap(Ws2_h, d * 128, [[1024, 128], [128 * 1024, G], [1, 128]]))
                        s_wl2[d % 2].inc(i, 16)

                @blk.tensor
                def _(eng):
                    for d in range(G):
                        eng.wait_ge(s_wl2[d % 2].h, s_wl2[d % 2].base + ((d // 2 + 1)) * 16)
                        for ts in range(2):
                            j = d * 2 + ts
                            if j >= 2:
                                eng.wait_ge(s_sc2.h, s_sc2.base + (j - 1))
                            for k in range(G):
                                i = eng.matmul(ps_y[j % 2][:],
                                               w2b[:, (d % 2) * 16 * 128 + k * 128:
                                                    (d % 2) * 16 * 128 + (k + 1) * 128],
                                               gs_sb[:, (k * 2 + ts) * TC:(k * 2 + ts + 1) * TC],
                                               start=(k == 0), stop=(k == G - 1))
                            s_mm2.inc(i)

                @blk.scalar
                def _(eng):
                    for d in range(G):
                        for ts in range(2):
                            j = d * 2 + ts
                            eng.wait_ge(s_mm2.h, s_mm2.base + (j + 1))
                            i = eng.activation(
                                y_acc[:, d * TB + ts * TC:d * TB + (ts + 1) * TC],
                                ps_y[j % 2][:], AF.Copy)
                            eng.drain()
                            s_sc2.inc(i)

            for i_e in range(E):
                p.stage_reset()
                s_wl3 = p.sem(tag, 'sw')
                s_wl3b = p.sem(tag, 'sw')
                s_wl3c = p.sem(tag, 'sw')
                s_mm3 = p.sem(tag)
                s_ev3 = p.sem(tag)
                s_vv3 = p.sem(tag)
                s_mk3 = p.sem(tag)
                s_pe5 = p.sem(tag)
                with nc.Block(no_gpsimd_drain=True) as blk:
                    @blk.gpsimd
                    def _(eng):
                        for hc in range(4):
                            i = eng.dma_start(
                                out=w1b[:, hc * G * 128:(hc + 1) * G * 128].rearrange(
                                    "p (k j) -> p k j", j=128),
                                in_=hap(W1_h, i_e * 1024 * H + hc * 128,
                                        [[H, 128], [128 * H, G], [1, 128]]))
                            s_wl3.inc(i, 16)
                        for hc in range(4):
                            i = eng.dma_start(
                                out=w3b[:, hc * G * 128:(hc + 1) * G * 128].rearrange(
                                    "p (k j) -> p k j", j=128),
                                in_=hap(W3_h, i_e * 1024 * H + hc * 128,
                                        [[H, 128], [128 * H, G], [1, 128]]))
                            s_wl3b.inc(i, 16)
                        for d in range(G):
                            i = eng.dma_start(
                                out=w2b[:, d * 4 * 128:(d + 1) * 4 * 128].rearrange(
                                    "p (k j) -> p k j", j=128),
                                in_=hap(W2_h, i_e * H * 1024 + d * 128,
                                        [[1024, 128], [128 * 1024, 4], [1, 128]]))
                            s_wl3c.inc(i, 16)

                    @blk.tensor
                    def _(eng):
                        for ts in range(2):
                            i = eng.matmul(ps_y[ts][:], sel32_sb[:, i_e * 128:(i_e + 1) * 128],
                                           route[0:32, tok0 + ts * TC:tok0 + (ts + 1) * TC],
                                           start=True, stop=True)
                        s_pe5.inc(i)
                        eng.wait_ge(s_wl3.h, s_wl3.base + (4 * 16))
                        for hc in range(4):
                            for ts in range(2):
                                j = hc * 2 + ts
                                if j >= 2:
                                    eng.wait_ge(s_ev3.h, s_ev3.base + (j - 1))
                                for k in range(G):
                                    i = eng.matmul(ps_a[j % 2][:],
                                                   w1b[:, hc * G * 128 + k * 128:
                                                        hc * G * 128 + (k + 1) * 128],
                                                   xs(k, ts), start=(k == 0), stop=(k == G - 1))
                                s_mm3.inc(i)
                                if j == 0:
                                    eng.wait_ge(s_wl3b.h, s_wl3b.base + (4 * 16))
                                if j >= 2:
                                    eng.wait_ge(s_vv3.h, s_vv3.base + (j - 1))
                                for k in range(G):
                                    i = eng.matmul(ps_b[j % 2][:],
                                                   w3b[:, hc * G * 128 + k * 128:
                                                        hc * G * 128 + (k + 1) * 128],
                                                   xs(k, ts), start=(k == 0), stop=(k == G - 1))
                                s_mm3.inc(i)
                        eng.wait_ge(s_wl3c.h, s_wl3c.base + (8 * 16))
                        for d in range(G):
                            for ts in range(2):
                                jj = d * 2 + ts
                                eng.wait_ge(s_vv3.h, s_vv3.base + (8))
                                if jj >= 2:
                                    eng.wait_ge(s_vv3.h, s_vv3.base + (8 + jj - 1))
                                for hc in range(4):
                                    i = eng.matmul(ps_y[jj % 2][:],
                                                   w2b[:, d * 4 * 128 + hc * 128:
                                                        d * 4 * 128 + (hc + 1) * 128],
                                                   g_sb[:, (hc * 2 + ts) * TC:
                                                        (hc * 2 + ts + 1) * TC],
                                                   start=(hc == 0), stop=(hc == 3))
                                s_mm3.inc(i)

                    @blk.scalar
                    def _(eng):
                        eng.wait_ge(s_pe5.h, s_pe5.base + (1))
                        for ts in range(2):
                            i = eng.activation(wmsk[:, ts * TC:(ts + 1) * TC], ps_y[ts][:],
                                               AF.Copy)
                        i = eng.drain()
                        s_mk3.inc(i)
                        for j in range(8):
                            eng.wait_ge(s_mm3.h, s_mm3.base + (2 * j + 1))
                            if j >= 2:
                                eng.wait_ge(s_vv3.h, s_vv3.base + (j - 1))
                            i = eng.activation(a_sb[:, (j % 2) * TC:(j % 2 + 1) * TC],
                                               ps_a[j % 2][:], AF.Silu)
                            eng.drain()
                            s_ev3.inc(i)

                    @blk.vector
                    def _(eng):
                        eng.wait_ge(s_mk3.h, s_mk3.base + (1))
                        for j in range(8):
                            eng.wait_ge(s_mm3.h, s_mm3.base + (2 * j + 2))
                            eng.wait_ge(s_ev3.h, s_ev3.base + (j + 1))
                            ts = j % 2
                            eng.tensor_mul(g_sb[:, j * TC:(j + 1) * TC],
                                           a_sb[:, (j % 2) * TC:(j % 2 + 1) * TC],
                                           ps_b[j % 2][:])
                            eng.drain()
                            i = eng.tensor_mul(g_sb[:, j * TC:(j + 1) * TC],
                                               g_sb[:, j * TC:(j + 1) * TC],
                                               wmsk[:, ts * TC:(ts + 1) * TC])
                            i = eng.drain()
                            s_vv3.inc(i)
                        for jj in range(16):
                            eng.wait_ge(s_mm3.h, s_mm3.base + (16 + jj + 1))
                            d, ts = divmod(jj, 2)
                            i = eng.scalar_tensor_tensor(
                                y_acc[:, d * TB + ts * TC:d * TB + (ts + 1) * TC],
                                ps_y[jj % 2][:], 1.0,
                                y_acc[:, d * TB + ts * TC:d * TB + (ts + 1) * TC],
                                ALU.mult, ALU.add)
                            i = eng.drain()
                            s_vv3.inc(i)

        with ExitStack() as cB2:
            eb2 = cB2.enter_context
            p.stage_reset()
            xld = eb2(nc.sbuf_tensor(f"{tag}b{blkid}_xld", [128, 2 * TC], F32))
            stg = eb2(nc.sbuf_tensor(f"{tag}b{blkid}_stg", [128, 2 * TC], F32))
            s_xl = [p.sem(tag, 'hw'), p.sem(tag, 'hw')]
            s_st = [p.sem(tag, 'hw'), p.sem(tag, 'hw')]
            s_ev2 = p.sem(tag)
            with nc.Block(no_gpsimd_drain=True) as blk:
                @blk.sync
                def _(eng):
                    for j in range(2 * G):
                        d, ts = divmod(j, 2)
                        if j >= 2:
                            eng.wait_ge(s_ev2.h, s_ev2.base + (j - 1))
                        i = eng.dma_start(
                            out=xld[:, (j % 2) * TC:(j % 2 + 1) * TC],
                            in_=xT_in[d * 128:(d + 1) * 128,
                                      tok0 + ts * TC:tok0 + (ts + 1) * TC])
                        s_xl[j % 2].inc(i, 16)
                        eng.wait_ge(s_ev2.h, s_ev2.base + (j + 1))
                        i = eng.dma_start(
                            out=xT_out[d * 128:(d + 1) * 128,
                                       tok0 + ts * TC:tok0 + (ts + 1) * TC],
                            in_=stg[:, (j % 2) * TC:(j % 2 + 1) * TC])
                        s_st[j % 2].inc(i, 16)
                    eng.wait_ge(s_st[0].h, s_st[0].v)
                    eng.wait_ge(s_st[1].h, s_st[1].v)

                @blk.vector
                def _(eng):
                    for j in range(2 * G):
                        d, ts = divmod(j, 2)
                        eng.wait_ge(s_xl[j % 2].h, s_xl[j % 2].base + ((j // 2 + 1)) * 16)
                        if j >= 2:
                            eng.wait_ge(s_st[j % 2].h, s_st[j % 2].base + ((j // 2)) * 16)
                        i = eng.tensor_add(stg[:, (j % 2) * TC:(j % 2 + 1) * TC],
                                           xld[:, (j % 2) * TC:(j % 2 + 1) * TC],
                                           y_acc[:, d * TB + ts * TC:d * TB + (ts + 1) * TC])
                        i = eng.drain()
                        s_ev2.inc(i)


MAMBA_TRUNC = None
MAMBA_NQ = 4
MAMBA_NG = 16
MAMBA_SCAN_PARTS = 'full'
MAMBA_FIN_MODE = 'full'


class _SkipScanT(Exception):
    pass


_SkipScan = _SkipScanT()


def emit_mamba(p, ctx, consts, tag, xT_in, xT_out, W, dbg=None):
    nc = p.nc
    DI, DS, DR = 2048, 16, 256
    GI = DI // 128
    in_proj_h = p.inp(f'{tag}_inproj', W['in_proj_w'])
    xp_pad = np.zeros((DI, 384), np.float32)
    xp_pad[:, :288] = W['x_proj_w']
    x_proj_h = p.inp(f'{tag}_xprojR', np.ascontiguousarray(
        xp_pad.reshape(16, 128, 3, 128).transpose(2, 1, 0, 3)))
    dt_w_h = p.inp(f'{tag}_dtw', W['dt_proj_w'])
    dt_b_h = p.inp(f'{tag}_dtb', bias_layout(W['dt_proj_b']))
    alog_h = p.inp(f'{tag}_alog', np.ascontiguousarray(
        W['A_log'].reshape(GI, 128, DS).transpose(1, 0, 2).reshape(128, GI * DS)))
    d_h = p.inp(f'{tag}_d', bias_layout(W['D']))
    nw_h = p.inp(f'{tag}_nw', bias_layout(W['norm_w']))
    nb_h = p.inp(f'{tag}_nb', bias_layout(W['norm_b']))
    out_proj_h = p.inp(f'{tag}_outprojR', np.ascontiguousarray(
        W['out_proj_w'].reshape(16, 128, 8, 128).transpose(2, 1, 0, 3)))

    bcT = ctx.enter_context(nc.sbuf_tensor(f"{tag}_bc", [32, T], BF16))
    xn0T = p.scratch((D, T), F32)
    uT = p.scratch((DI, T), BF16)
    szT = p.scratch((DI, T), BF16)
    udT = p.scratch((DI, T), BF16)
    deltaT = p.scratch((DI, T), F32)
    wTs = p.scratch((DI, T), BF16)
    ySp = p.scratch((DI, T), F32)
    y2T = p.scratch((DI, T), BF16)

    if dbg is not None:
        dbg.update(xn0T=xn0T, uT=uT, szT=szT, udT=udT, deltaT=deltaT, wTs=wTs,
                   ySp=ySp, y2T=y2T)
    with ExitStack() as c0:
        p.stage_reset()
        emit_ln(p, c0, consts, f"{tag}_ln1", xT_in, None, hbm_out=xn0T)

    if MAMBA_TRUNC == 'ln':
        _dummy_copy(p, ctx, consts, tag + '_dc', xT_in, xT_out)
        return
    with ExitStack() as c1:
        p.stage_reset()
        XN2 = c1.enter_context(nc.sbuf_tensor(f"{tag}_xn2", [128, G * T], BF16))
        (nw_sb, nb_sb) = load_small_sbuf(p, c1, f"{tag}_nwb",
                                         [(nw_h, (128, 8), F32), (nb_h, (128, 8), F32)])
        with ExitStack() as cl2:
            emit_ln(p, cl2, consts, f"{tag}_ln2", xn0T, XN2, gain_beta=(nw_sb, nb_sb))

        p.stage_reset()
        def ip_out(m, tc):
            if m < 16:
                return uT[m * 128:(m + 1) * 128, tc * TC:(tc + 1) * TC]
            return szT[(m - 16) * 128:(m - 16 + 1) * 128, tc * TC:(tc + 1) * TC]
        emit_proj_hbm(p, c1, consts, f"{tag}_ip", XN2, in_proj_h, 1024, 4096,
                      ip_out, act=AF.Silu)

    if MAMBA_TRUNC == 'ip':
        _dummy_copy(p, ctx, consts, tag + '_dc', xT_in, xT_out)
        return
    with ExitStack() as c2:
        p.stage_reset()
        U = c2.enter_context(nc.sbuf_tensor(f"{tag}_u", [128, GI * T], BF16))
        drT = c2.enter_context(nc.sbuf_tensor(f"{tag}_dr", [128, 2 * T], BF16))
        (d_sb,) = load_small_sbuf(p, c2, f"{tag}_dsb", [(d_h, (128, 16), F32)])
        s_ul = p.sem(tag, 'hw')
        s_ud = [p.sem(tag, 'hw'), p.sem(tag, 'hw')]
        s_sc = p.sem(tag)
        with nc.Block(no_gpsimd_drain=True) as blk:
            @blk.sync
            def _(eng):
                for gi in range(GI):
                    i = eng.dma_start(out=U[:, gi * T:(gi + 1) * T],
                                      in_=uT[gi * 128:(gi + 1) * 128, :])
                    s_ul.inc(i, 16)
                eng.wait_ge(s_ul.h, s_ul.v)
        with nc.Block(no_gpsimd_drain=True) as blk:
            @blk.scalar
            def _(eng):
                for gi in range(GI):
                    if gi >= 2:
                        eng.wait_ge(s_ud[gi % 2].h, s_ud[gi % 2].base + ((gi // 2)) * 16)
                    i = eng.activation(drT[:, (gi % 2) * T:(gi % 2 + 1) * T],
                                       U[:, gi * T:(gi + 1) * T], AF.Identity,
                                       scale=d_sb[:, gi:gi + 1])
                    eng.drain()
                    s_sc.inc(i)

            @blk.sync
            def _(eng):
                for gi in range(GI):
                    eng.wait_ge(s_sc.h, s_sc.base + (gi + 1))
                    i = eng.dma_start(out=udT[gi * 128:(gi + 1) * 128, :],
                                      in_=drT[:, (gi % 2) * T:(gi % 2 + 1) * T])
                    s_ud[gi % 2].inc(i, 16)
                eng.wait_ge(s_ud[0].h, s_ud[0].v)
                eng.wait_ge(s_ud[1].h, s_ud[1].v)

        if MAMBA_TRUNC == 'ud':
            _dummy_copy(p, c2, consts, tag + '_dc', xT_in, xT_out)
            return
        p.stage_reset()

        def xdb_out_cb(eng, m, tc, ps):
            if m < 2:
                i = eng.activation(drT[:, m * T + tc * TC:m * T + (tc + 1) * TC],
                                   ps[:], AF.Copy)
                eng.drain()
                return i
            i = eng.activation(bcT[0:32, tc * TC:(tc + 1) * TC], ps[0:32, :], AF.Copy)
            eng.drain()
            return i

        emit_proj(p, c2, consts, f"{tag}_xdb", U, x_proj_h, DI, 384, xdb_out_cb,
                  rhs=lambda k, tc: U[:, k * T + tc * TC: k * T + (tc + 1) * TC],
                  packed=True)

        if MAMBA_TRUNC == 'xdb':
            _dummy_copy(p, c2, consts, tag + '_dc', xT_in, xT_out)
            return
        p.stage_reset()
        (dtb_sb,) = load_small_sbuf(p, c2, f"{tag}_dtb2", [(dt_b_h, (128, 16), F32)])
        dwb = c2.enter_context(nc.sbuf_tensor(f"{tag}_dwb", [128, 2 * 2 * 128], BF16))
        vsb = c2.enter_context(nc.sbuf_tensor(f"{tag}_vsb", [128, 6 * TC], F32))
        wstg = c2.enter_context(nc.sbuf_tensor(f"{tag}_wstg", [128, 2 * TC], BF16))
        pss = [c2.enter_context(nc.psum_tensor(f"{tag}_psd{i}", [128, TC], F32))
               for i in range(2)]
        s_wl = [p.sem(tag, 'sw'), p.sem(tag, 'sw')]
        s_mm = p.sem(tag)
        s_sp = p.sem(tag)
        s_ev = p.sem(tag)
        s_vv = p.sem(tag)
        s_st = [p.sem(tag, 'hw'), p.sem(tag, 'hw')]
        with nc.Block(no_gpsimd_drain=True) as blk:
            @blk.gpsimd
            def _(eng):
                for m in range(GI):
                    if m >= 2:
                        eng.wait_ge(s_mm.h, s_mm.base + ((m - 1)) * NTC)
                    i = eng.dma_start(
                        out=dwb[:, (m % 2) * 256:(m % 2 + 1) * 256].rearrange(
                            "p (k j) -> p k j", j=128),
                        in_=hap(dt_w_h, m * 128, [[DI, 128], [128 * DI, 2], [1, 128]]))
                    s_wl[m % 2].inc(i, 16)

            @blk.tensor
            def _(eng):
                for m in range(GI):
                    eng.wait_ge(s_wl[m % 2].h, s_wl[m % 2].base + ((m // 2 + 1)) * 16)
                    for tc in range(NTC):
                        j = m * NTC + tc
                        if j >= 2:
                            eng.wait_ge(s_ev.h, s_ev.base + (j - 1))
                        for k in range(2):
                            i = eng.matmul(pss[j % 2][:],
                                           dwb[:, (m % 2) * 256 + k * 128:
                                               (m % 2) * 256 + (k + 1) * 128],
                                           drT[:, k * T + tc * TC:k * T + (tc + 1) * TC],
                                           start=(k == 0), stop=(k == 1))
                        s_mm.inc(i)

            @blk.scalar
            def _(eng):
                for m in range(GI):
                    for tc in range(NTC):
                        j = m * NTC + tc
                        eng.wait_ge(s_mm.h, s_mm.base + (j + 1))
                        if j >= 2:
                            eng.wait_ge(s_vv.h, s_vv.base + (j - 1))
                            eng.wait_ge(s_st[j % 2].h, s_st[j % 2].base + ((j // 2)) * 32)
                        sl = (j % 2) * 3 * TC
                        i = eng.activation(vsb[:, sl:sl + TC], pss[j % 2][:], AF.Identity,
                                           bias=dtb_sb[:, m:m + 1])
                        i = eng.drain()
                        s_ev.inc(i)
                        eng.activation(vsb[:, sl + TC:sl + 2 * TC], vsb[:, sl:sl + TC],
                                       AF.Abs)
                        eng.drain()
                        eng.activation(vsb[:, sl + 2 * TC:sl + 3 * TC],
                                       vsb[:, sl + TC:sl + 2 * TC], AF.Exp, scale=-1.0)
                        eng.drain()
                        i = eng.activation(vsb[:, sl + TC:sl + 2 * TC],
                                           vsb[:, sl + 2 * TC:sl + 3 * TC], AF.Ln, bias=1.0)
                        eng.drain()
                        s_sp.inc(i)

            @blk.vector
            def _(eng):
                for m in range(GI):
                    for tc in range(NTC):
                        j = m * NTC + tc
                        eng.wait_ge(s_sp.h, s_sp.base + (j + 1))
                        if j >= 2:
                            eng.wait_ge(s_st[j % 2].h, s_st[j % 2].base + ((j // 2)) * 32)
                        sl = (j % 2) * 3 * TC
                        eng.tensor_relu(vsb[:, sl:sl + TC], vsb[:, sl:sl + TC])
                        eng.drain()
                        eng.tensor_add(vsb[:, sl:sl + TC], vsb[:, sl:sl + TC],
                                       vsb[:, sl + TC:sl + 2 * TC])
                        eng.drain()
                        i = eng.tensor_mul(wstg[:, (j % 2) * TC:(j % 2 + 1) * TC],
                                           vsb[:, sl:sl + TC],
                                           U[:, m * T + tc * TC:m * T + (tc + 1) * TC])
                        i = eng.drain()
                        s_vv.inc(i)

            @blk.sync
            def _(eng):
                for m in range(GI):
                    for tc in range(NTC):
                        j = m * NTC + tc
                        eng.wait_ge(s_vv.h, s_vv.base + (j + 1))
                        sl = (j % 2) * 3 * TC
                        i = eng.dma_start(
                            out=deltaT[m * 128:(m + 1) * 128, tc * TC:(tc + 1) * TC],
                            in_=vsb[:, sl:sl + TC])
                        s_st[j % 2].inc(i, 16)
                        i = eng.dma_start(
                            out=wTs[m * 128:(m + 1) * 128, tc * TC:(tc + 1) * TC],
                            in_=wstg[:, (j % 2) * TC:(j % 2 + 1) * TC])
                        s_st[j % 2].inc(i, 16)
                eng.wait_ge(s_st[0].h, s_st[0].v)
                eng.wait_ge(s_st[1].h, s_st[1].v)

    if MAMBA_TRUNC == 'delta':
        _dummy_copy(p, ctx, consts, tag + '_dc', xT_in, xT_out)
        return
    try:
      with ExitStack() as c3:
        p.stage_reset()
        e3 = c3.enter_context
        alog_sb, = load_small_sbuf(p, c3, f"{tag}_al", [(alog_h, (128, GI * DS), F32)])
        A_sb = e3(nc.sbuf_tensor(f"{tag}_A", [128, GI * DS], F32))
        with nc.Block(no_gpsimd_drain=True) as blk:
            @blk.scalar
            def _(eng):
                eng.activation(A_sb[:], alog_sb[:], AF.Exp)
            @blk.vector
            def _(eng):
                pass
        with nc.Block(no_gpsimd_drain=True) as blk:
            @blk.vector
            def _(eng):
                eng.tensor_scalar_mul(A_sb[:], A_sb[:], -1.0)

        sel32_h2 = p.inp('c_sel32', np.kron(np.eye(32, dtype=np.float32),
                                            np.ones((1, 128), np.float32)))
        (sel32m_sb,) = load_small_sbuf(p, c3, f"{tag}_s32", [(sel32_h2, (32, 32 * 128), BF16)])
        if MAMBA_SCAN_PARTS in ('none', 'asb'):
            raise _SkipScan
        rep = e3(nc.sbuf_tensor(f"{tag}_rep", [128, 8 * T], BF16))
        dg = e3(nc.sbuf_tensor(f"{tag}_dg", [128, T], F32))
        wg = e3(nc.sbuf_tensor(f"{tag}_wg", [128, T], BF16))
        dA = e3(nc.sbuf_tensor(f"{tag}_dA", [128, 2 * T], F32))
        bterm = e3(nc.sbuf_tensor(f"{tag}_bt", [128, 2 * T], BF16))
        hh = e3(nc.sbuf_tensor(f"{tag}_hh", [128, 2 * T], BF16))
        hc = e3(nc.sbuf_tensor(f"{tag}_hc", [128, T], BF16))
        ysg = e3(nc.sbuf_tensor(f"{tag}_ysg", [128, T], F32))
        ps_y = [e3(nc.psum_tensor(f"{tag}_psy{i}", [128, TC], F32)) for i in range(8)]
        ps_r = [ps_y[0], ps_y[1]]

        for q in range(MAMBA_NQ):
            p.stage_reset()
            s_pe = p.sem(tag)
            s_scq = p.sem(tag)
            with nc.Block(no_gpsimd_drain=True) as blk:
                @blk.tensor
                def _(eng):
                    for jn in range(4):
                        n = q * 4 + jn
                        for src_i, row in ((0, n), (1, 16 + n)):
                            for tc in range(NTC):
                                jj = (jn * 2 + src_i) * NTC + tc
                                if jj >= 2:
                                    eng.wait_ge(s_scq.h, s_scq.base + (jj - 1))
                                i = eng.matmul(ps_r[jj % 2][:],
                                               sel32m_sb[:, row * 128:(row + 1) * 128],
                                               bcT[0:32, tc * TC:(tc + 1) * TC],
                                               start=True, stop=True)
                                s_pe.inc(i)

                @blk.scalar
                def _(eng):
                    for jn in range(4):
                        for src_i in range(2):
                            for tc in range(NTC):
                                jj = (jn * 2 + src_i) * NTC + tc
                                eng.wait_ge(s_pe.h, s_pe.base + (jj + 1))
                                i = eng.activation(
                                    rep[:, (jn * 2 + src_i) * T + tc * TC:
                                        (jn * 2 + src_i) * T + (tc + 1) * TC],
                                    ps_r[jj % 2][:], AF.Copy)
                                eng.drain()
                                s_scq.inc(i)

            if MAMBA_SCAN_PARTS == 'rep':
                continue
            for g in range(MAMBA_NG):
                p.stage_reset()
                s_ld = p.sem(tag, 'hw')
                s_yl = p.sem(tag, 'hw')
                s_sc2 = p.sem(tag)
                s_v2 = p.sem(tag)
                s_g2 = p.sem(tag)
                s_bD = p.sem(tag)
                s_bG = p.sem(tag)
                s_hD = p.sem(tag)
                s_hG = p.sem(tag)
                s_pe2 = p.sem(tag)
                sl = g % 2
                with nc.Block(no_gpsimd_drain=True) as blk:
                    @blk.sync
                    def _(eng):
                        i = eng.dma_start(out=dg[:, :],
                                          in_=deltaT[g * 128:(g + 1) * 128, :])
                        s_ld.inc(i, 16)
                        i = eng.dma_start(out=wg[:, :],
                                          in_=wTs[g * 128:(g + 1) * 128, :])
                        s_ld.inc(i, 16)
                        if q > 0:
                            i = eng.dma_start(out=ysg[:, :],
                                              in_=ySp[g * 128:(g + 1) * 128, :])
                            s_yl.inc(i, 16)
                        eng.wait_ge(s_ld.h, s_ld.v)
                        if q > 0:
                            eng.wait_ge(s_yl.h, s_yl.v)

                    @blk.scalar
                    def _(eng):
                        eng.wait_ge(s_ld.h, s_ld.v)
                        for jn in range(4):
                            n = q * 4 + jn
                            if jn >= 2:
                                eng.wait_ge(s_g2.h, s_g2.base + jn - 1)
                            i = eng.activation(dA[:, (jn % 2) * T:(jn % 2 + 1) * T],
                                               dg[:, :], AF.Exp,
                                               scale=A_sb[:, g * DS + n:g * DS + n + 1])
                            eng.drain()
                            s_sc2.inc(i)

                    @blk.vector
                    def _(eng):
                        eng.wait_ge(s_ld.h, s_ld.v)
                        if q > 0:
                            eng.wait_ge(s_yl.h, s_yl.v)
                        for jn in range(4):
                            if jn % 2 == 0:
                                if jn >= 2:
                                    eng.wait_ge(s_g2.h, s_g2.base + jn - 1)
                                i = eng.tensor_mul(bterm[:, (jn % 2) * T:(jn % 2 + 1) * T],
                                                   wg[:, :],
                                                   rep[:, (jn * 2) * T:(jn * 2 + 1) * T])
                                i = eng.drain()
                                s_bD.inc(i)
                            eng.wait_ge(s_sc2.h, s_sc2.base + jn + 1)
                            if jn % 2 == 0:
                                pass
                            else:
                                eng.wait_ge(s_bG.h, s_bG.base + (jn + 1) // 2)
                            if jn >= 2:
                                if jn % 2 == 0:
                                    eng.wait_ge(s_hD.h, s_hD.base + jn // 2)
                                else:
                                    eng.wait_ge(s_hG.h, s_hG.base + (jn - 1) // 2)
                            i = eng.tensor_tensor_scan(
                                hh[:, (jn % 2) * T:(jn % 2 + 1) * T],
                                dA[:, (jn % 2) * T:(jn % 2 + 1) * T],
                                bterm[:, (jn % 2) * T:(jn % 2 + 1) * T],
                                0.0, ALU.mult, ALU.add)
                            i = eng.drain()
                            s_g2.inc(i)
                            if jn % 2 == 0:
                                eng.wait_ge(s_pe2.h, s_pe2.base + jn * NTC)
                                i = eng.tensor_mul(hc[:, :],
                                                   hh[:, (jn % 2) * T:(jn % 2 + 1) * T],
                                                   rep[:, (jn * 2 + 1) * T:(jn * 2 + 2) * T])
                                i = eng.drain()
                                s_hD.inc(i)
                        eng.wait_ge(s_pe2.h, s_pe2.base + 4 * NTC)
                        for c in range(NTC):
                            if q == 0:
                                eng.tensor_copy(ysg[:, c * TC:(c + 1) * TC], ps_y[c][:])
                            else:
                                eng.tensor_add(ysg[:, c * TC:(c + 1) * TC],
                                               ysg[:, c * TC:(c + 1) * TC], ps_y[c][:])
                        i = eng.drain()
                        s_v2.inc(i)

                    @blk.gpsimd
                    def _(eng):
                        eng.wait_ge(s_ld.h, s_ld.v)
                        for jn in range(1, 4, 2):
                            if jn >= 2:
                                eng.wait_ge(s_g2.h, s_g2.base + jn - 1)
                            i = eng.tensor_mul(bterm[:, (jn % 2) * T:(jn % 2 + 1) * T],
                                               wg[:, :],
                                               rep[:, (jn * 2) * T:(jn * 2 + 1) * T])
                            i = eng.drain()
                            s_bG.inc(i)
                            eng.wait_ge(s_g2.h, s_g2.base + jn + 1)
                            eng.wait_ge(s_pe2.h, s_pe2.base + jn * NTC)
                            i = eng.tensor_mul(hc[:, :],
                                               hh[:, (jn % 2) * T:(jn % 2 + 1) * T],
                                               rep[:, (jn * 2 + 1) * T:(jn * 2 + 2) * T])
                            i = eng.drain()
                            s_hG.inc(i)

                    @blk.tensor
                    def _(eng):
                        for jn in range(4):
                            if jn % 2 == 0:
                                eng.wait_ge(s_hD.h, s_hD.base + jn // 2 + 1)
                            else:
                                eng.wait_ge(s_hG.h, s_hG.base + (jn + 1) // 2)
                            for c in range(NTC):
                                i = eng.matmul(ps_y[c][:], consts.ident[:],
                                               hc[:, c * TC:(c + 1) * TC],
                                               start=(jn == 0), stop=(jn == 3))
                                s_pe2.inc(i)

                with nc.Block(no_gpsimd_drain=True) as blk:
                    @blk.sync
                    def _(eng):
                        sst = p.sem(tag, 'hw')
                        eng.wait_ge(s_v2.h, s_v2.base + (1))
                        i = eng.dma_start(out=ySp[g * 128:(g + 1) * 128, :],
                                          in_=ysg[:, :])
                        sst.inc(i, 16)
                        eng.wait_ge(sst.h, sst.v)

    except _SkipScanT:
        pass
    if MAMBA_TRUNC == 'scan':
        _dummy_copy(p, ctx, consts, tag + '_dc', xT_in, xT_out)
        return
    if MAMBA_FIN_MODE == 'dummy_sems':
        with ExitStack() as cD:
            t = cD.enter_context(nc.sbuf_tensor(f"{tag}_dt", [128, T], F32))
            with nc.Block(no_gpsimd_drain=True) as blk:
                @blk.sync
                def _(eng):
                    for g in range(G):
                        s = p.sem(tag, 'hw')
                        i = eng.dma_start(out=t[:], in_=xT_in[g * 128:(g + 1) * 128, :])
                        s.inc(i, 16)
                        eng.wait_ge(s.h, s.v)
                        i = eng.dma_start(out=xT_out[g * 128:(g + 1) * 128, :], in_=t[:])
                        s.inc(i, 16)
                        eng.wait_ge(s.h, s.v)
        return
    if MAMBA_FIN_MODE == 'dummy_bs':
        with ExitStack() as cD:
            t = cD.enter_context(nc.sbuf_tensor(f"{tag}_fa", [128, T], F32))
            fb_u = cD.enter_context(nc.sbuf_tensor(f"{tag}_fb", [128, T], BF16))
            fc_u = cD.enter_context(nc.sbuf_tensor(f"{tag}_fc", [128, T], BF16))
            fo_u = cD.enter_context(nc.sbuf_tensor(f"{tag}_fo", [128, T], BF16))
            for g in range(16):
                gg = g % 8
                sD = p.sem(tag, 'hw')
                sE = p.sem(tag, 'hw')
                sV_unused = p.sem(tag)
                with nc.Block(no_gpsimd_drain=True) as blk:
                    @blk.sync
                    def _(eng):
                        i = eng.dma_start(out=t[:], in_=xT_in[gg * 128:(gg + 1) * 128, :])
                        sD.inc(i, 16)
                        eng.wait_ge(sD.h, sD.v)
                        i = eng.dma_start(out=xT_out[gg * 128:(gg + 1) * 128, :], in_=t[:])
                        sE.inc(i, 16)
                        eng.wait_ge(sE.h, sE.v)
        return
    if MAMBA_FIN_MODE == 'dummy_blocks':
        with ExitStack() as cD:
            t = cD.enter_context(nc.sbuf_tensor(f"{tag}_dt", [128, T], F32))
            sD = p.sem(tag, 'hw')
            for g in range(G):
                with nc.Block(no_gpsimd_drain=True) as blk:
                    @blk.sync
                    def _(eng):
                        i = eng.dma_start(out=t[:], in_=xT_in[g * 128:(g + 1) * 128, :])
                        sD.inc(i, 16)
                        eng.wait_ge(sD.h, sD.v)
                        i = eng.dma_start(out=xT_out[g * 128:(g + 1) * 128, :], in_=t[:])
                        sD.inc(i, 16)
                        eng.wait_ge(sD.h, sD.v)
        return
    with ExitStack() as c4:
        e4 = c4.enter_context
        fa = e4(nc.sbuf_tensor(f"{tag}_fa", [128, T], F32))
        fb = e4(nc.sbuf_tensor(f"{tag}_fb", [128, T], BF16))
        fc = e4(nc.sbuf_tensor(f"{tag}_fc", [128, T], BF16))
        fo = e4(nc.sbuf_tensor(f"{tag}_fo", [128, T], BF16))
        for g in range(GI):
            s_l4 = p.sem(tag, 'hw')
            s_s4 = p.sem(tag, 'hw')
            s_v4 = p.sem(tag)
            with nc.Block(no_gpsimd_drain=True) as blk:
                @blk.sync
                def _(eng):
                    nl = {'copy1': 1, 'copy3': 3, 'full': 3, 'copyx': 1}[MAMBA_FIN_MODE]
                    srcT = xT_in if MAMBA_FIN_MODE == 'copyx' else ySp
                    i = eng.dma_start(out=fa[:, :], in_=srcT[(g % 8) * 128:(g % 8 + 1) * 128, :]
                                      if MAMBA_FIN_MODE == 'copyx' else
                                      ySp[g * 128:(g + 1) * 128, :])
                    s_l4.inc(i, 16)
                    if nl == 3:
                        i = eng.dma_start(out=fb[:, :], in_=udT[g * 128:(g + 1) * 128, :])
                        s_l4.inc(i, 16)
                        i = eng.dma_start(out=fc[:, :], in_=szT[g * 128:(g + 1) * 128, :])
                        s_l4.inc(i, 16)
                    if MAMBA_FIN_MODE == 'full':
                        eng.wait_ge(s_v4.h, s_v4.base + 1)
                        i = eng.dma_start(out=y2T[g * 128:(g + 1) * 128, :], in_=fo[:, :])
                    else:
                        eng.wait_ge(s_l4.h, s_l4.v)
                        dstT = xT_out if MAMBA_FIN_MODE == 'copyx' else ySp
                        i = eng.dma_start(out=dstT[(g % 8) * 128:(g % 8 + 1) * 128, :]
                                          if MAMBA_FIN_MODE == 'copyx' else
                                          ySp[g * 128:(g + 1) * 128, :], in_=fa[:, :])
                    s_s4.inc(i, 16)
                    eng.wait_ge(s_s4.h, s_s4.v)

                if MAMBA_FIN_MODE == 'full':
                    @blk.vector
                    def _(eng):
                        eng.wait_ge(s_l4.h, s_l4.base + 48)
                        eng.tensor_add(fa[:, :], fa[:, :], fb[:, :])
                        eng.drain()
                        i = eng.tensor_mul(fo[:, :], fa[:, :], fc[:, :])
                        i = eng.drain()
                        s_v4.inc(i)

    with ExitStack() as c5:
        p.stage_reset()
        Y2 = c5.enter_context(nc.sbuf_tensor(f"{tag}_y2", [128, GI * T], BF16))
        s_l5 = p.sem(tag, 'hw')
        with nc.Block(no_gpsimd_drain=True) as blk:
            @blk.sync
            def _(eng):
                for gi in range(GI):
                    i = eng.dma_start(out=Y2[:, gi * T:(gi + 1) * T],
                                      in_=y2T[gi * 128:(gi + 1) * 128, :])
                    s_l5.inc(i, 16)
                eng.wait_ge(s_l5.h, s_l5.v)
        p.stage_reset()
        emit_proj_residual(p, c5, consts, f"{tag}_op", out_proj_h, DI,
                           lambda k, tc: Y2[:, k * T + tc * TC: k * T + (tc + 1) * TC],
                           xT_in, xT_out, xT_in2=xn0T, packed=True)



def emit_stage(p, consts, s, inputs, xin, xout):
    g = lambda k: np.asarray(inputs[k], np.float32)
    if s == 0:
        with ExitStack() as c:
            emit_l0a(p, c, consts, 's0', xin, xout, {
                'Wqkv': g('L0_attn_a_Wqkv'), 'bqkv': g('L0_attn_a_bqkv'),
                'Wproj': g('L0_attn_a_Wproj'), 'bproj': g('L0_attn_a_bproj')})
    elif s in (1, 3):
        moe1W = {k: g('L0_ffn_' + k) for k in ('gate_w', 'gate_b', 'w1', 'b1', 'w2', 'b2')}
        with ExitStack() as c:
            emit_moe1(p, c, consts, f's{s}', xin, xout, moe1W, 'moe1')
    elif s == 2:
        with ExitStack() as c:
            emit_mamba(p, c, consts, 's2', xin, xout, {
                k: g('L0_attn_b_' + k) for k in
                ('norm_w', 'norm_b', 'in_proj_w', 'x_proj_w', 'dt_proj_w',
                 'dt_proj_b', 'A_log', 'D', 'out_proj_w')})
    elif s == 4:
        with ExitStack() as c:
            emit_l1a(p, c, consts, 's4', xin, xout, {
                'qkv_w': g('L1_attn_a_qkv_w'), 'qkv_b': g('L1_attn_a_qkv_b'),
                'proj_w': g('L1_attn_a_proj_w'), 'proj_b': g('L1_attn_a_proj_b')})
    elif s in (5, 7):
        moe2W = {k: g('L1_ffn_' + k) for k in ('Wg', 'W1', 'W2', 'W3', 'Ws1', 'Ws2', 'Ws3')}
        with ExitStack() as c:
            emit_moe2(p, c, consts, f's{s}', xin, xout, moe2W, 'moe2')
    elif s == 6:
        with ExitStack() as c:
            emit_l1b(p, c, consts, 's6', xin, xout, {
                k: g('L1_attn_b_' + k) for k in
                ('basis_w', 'core', 'head_residual', 'v_w', 'v_b', 'o_w', 'o_b')})


def build_group(p, inputs, stages, x_percore):
    xT_in = p.inp_core('xT', x_percore)
    outT = p.out('outT', (D, T))
    xa = p.scratch((D, T), F32, name='xa')
    xb = p.scratch((D, T), F32, name='xb')
    with ExitStack() as ctx:
        consts = Consts(p, ctx)
        cur = xT_in
        for i, s in enumerate(stages):
            last = (i == len(stages) - 1)
            nxt = outT if last else (xa if cur is not xa else xb)
            emit_stage(p, consts, s, inputs, cur, nxt)
            if not last:
                p.stage_reset()
            cur = nxt
    return p


GROUPS = [[0, 1, 2, 3, 4, 5, 6, 7]]


def kernel(**inputs):
    x = np.asarray(inputs['L0_attn_a_x'], np.float32)
    cur = [np.ascontiguousarray(x[b].T) for b in range(NCORE)]
    for grp in GROUPS:
        p = Prog()
        build_group(p, inputs, grp, cur)
        res = run_bass_kernel_spmd(p.nc, p.in_maps(), core_ids=list(range(NCORE)))
        cur = [np.ascontiguousarray(np.asarray(r['outT'])) for r in res.results]
    out = np.stack([c.T for c in cur])
    return out.astype(np.float32)
